# version 26
# baseline (speedup 1.0000x reference)
import contextlib
import numpy as np
import ml_dtypes
import concourse.bass as bass
import concourse.mybir as mybir
from concourse.bass_utils import run_bass_kernel_spmd

F32 = mybir.dt.float32
BF16 = mybir.dt.bfloat16
I32 = mybir.dt.int32
AF = mybir.ActivationFunctionType
ALU = mybir.AluOpType
AX = mybir.AxisListType

L, D, R, DH, FF = 4096, 2048, 2048, 512, 8192
LS = 512
NLS = L // LS
DT, RT, HTN, FT = D // 128, R // 128, DH // 128, FF // 128
NCORES = 8
EPS = 1e-5
GELU_C = 0.044715
GELU_S = 2.0 * 0.7978845608028654
SGN_MASK = -0x80000000
EXP_MASK = 0x7F800000
INV_BASE = 0x7F000000
MIDS = [0.25, 0.75, 1.25, 1.75, 2.5, 3.5, 5.0]

ENGS = ("sync", "tensor", "vector", "scalar", "gpsimd")


class Trk:
    __slots__ = ("h", "c", "name")

    def __init__(self, h, name):
        self.h = h
        self.c = 0
        self.name = name


class Sched:

    def __init__(self, vtrk=None):
        self.ops = {k: [] for k in ENGS}
        self.vtrk = vtrk

    def add(self, eng, fn, waits=(), inc=None, n=1):
        ws = [(t, v) for (t, v) in waits if v > 0]
        if eng == "vector" and self.vtrk is not None:
            if self.vtrk.c > 0:
                ws.append((self.vtrk, self.vtrk.c))
            if inc is None:
                inc, n = self.vtrk, 1
            elif inc is not self.vtrk:
                raise ValueError("vector ops must inc the vector track")
            else:
                n = 1
        if inc is not None:
            inc.c += n
        self.ops[eng].append((fn, ws, inc, n))


def aft(trk):
    return (trk, trk.c)


def build_nc():
    nc = bass.Bass("TRN2", target_bir_lowering=False)
    ap = lambda t: t.ap()

    d_in = {}
    for name, shape, dt in [
        ("x_tok", [L, D], F32),
        ("xT", [D, L], F32),
        ("W2u", [16, 128, 16, 128], F32),
        ("W2v", [16, 128, 16, 128], F32),
        ("Wg1T", [D, DH], F32),
        ("Wg2T", [DH, R], F32),
        ("upT_sl", [FF // 8, D], F32),
        ("dnT_sl", [D // 8, FF], F32),
        ("bu_p", [128, 16], F32),
        ("bv_p", [128, 16], F32),
        ("bg1_p", [128, 4], F32),
        ("bg2_p", [128, 16], F32),
        ("cb_p", [128, 16], F32),
        ("db_p", [128, 16], F32),
        ("ub_p", [128, 64], F32),
        ("cw_p", [128, 48], F32),
        ("ident", [128, 128], BF16),
    ]:
        d_in[name] = ap(nc.dram_tensor(name, shape, dt, kind="ExternalInput"))
    d_out = ap(nc.dram_tensor("outT", [D, L], F32, kind="ExternalOutput"))

    xnT_scr = ap(nc.dram_tensor("xnT_scr", [32, 16, 128, 128], BF16))
    s_scr = ap(nc.dram_tensor("s_scr", [128, 16, L], F32))
    w2u_scr = ap(nc.dram_tensor("w2u_scr", [16, 128, 2048], BF16))
    w2v_scr = ap(nc.dram_tensor("w2v_scr", [16, 128, 2048], BF16))
    q_up_in = ap(nc.dram_tensor("q_up_in", [1024, 2048], BF16))
    q_dn_in = ap(nc.dram_tensor("q_dn_in", [256, 8192], BF16))
    q_up = ap(nc.dram_tensor("q_up", [8192, 2048], BF16))
    q_dn = ap(nc.dram_tensor("q_dn", [2048, 8192], BF16))

    with contextlib.ExitStack() as stack:
        e = stack.enter_context

        tS = Trk(e(nc.semaphore("tS")), "tS")
        tWU = [Trk(e(nc.semaphore(f"tWU{i}")), f"tWU{i}") for i in range(3)]
        tWV = [Trk(e(nc.semaphore(f"tWV{i}")), f"tWV{i}") for i in range(3)]
        tWZ = [Trk(e(nc.semaphore(f"tWZ{i}")), f"tWZ{i}") for i in range(3)]
        tUPs = [Trk(e(nc.semaphore(f"tUPs{i}")), f"tUPs{i}") for i in range(3)]
        tDNs = [Trk(e(nc.semaphore(f"tDNs{i}")), f"tDNs{i}") for i in range(2)]
        tXN = Trk(e(nc.semaphore("tXN")), "tXN")
        tXF = Trk(e(nc.semaphore("tXF")), "tXF")
        tSF = Trk(e(nc.semaphore("tSF")), "tSF")
        tQS = Trk(e(nc.semaphore("tQS")), "tQS")
        tACC = Trk(e(nc.semaphore("tACC")), "tACC")
        tCV = [Trk(e(nc.semaphore(f"tCV{i}")), f"tCV{i}") for i in range(2)]
        tCS = [Trk(e(nc.semaphore(f"tCS{i}")), f"tCS{i}") for i in range(2)]
        tG = Trk(e(nc.semaphore("tG")), "tG")
        tC = Trk(e(nc.semaphore("tC")), "tC")
        tT = Trk(e(nc.semaphore("tT")), "tT")
        tV = Trk(e(nc.semaphore("tV")), "tV")
        tSc = Trk(e(nc.semaphore("tSc")), "tSc")
        tG2 = Trk(e(nc.semaphore("tG2")), "tG2")

        sched = Sched(vtrk=tV)
        S = sched.add

        pH = e(nc.psum_tensor("pH", [128, 1024], F32))
        pU = [e(nc.psum_tensor(f"pU{i}", [128, 512], F32)) for i in range(2)]
        pV = [e(nc.psum_tensor(f"pV{i}", [128, 512], F32)) for i in range(2)]
        pG = [e(nc.psum_tensor(f"pG{i}", [128, 512], F32)) for i in range(2)]

        ident = e(nc.sbuf_tensor("identb", [128, 128], BF16))
        bu_p = e(nc.sbuf_tensor("bu_pb", [128, 16], F32))
        bv_p = e(nc.sbuf_tensor("bv_pb", [128, 16], F32))
        bg1_p = e(nc.sbuf_tensor("bg1_pb", [128, 4], F32))
        bg2_p = e(nc.sbuf_tensor("bg2_pb", [128, 16], F32))
        cb_p = e(nc.sbuf_tensor("cb_pb", [128, 16], F32))
        db_p = e(nc.sbuf_tensor("db_pb", [128, 16], F32))
        ub_p = e(nc.sbuf_tensor("ub_pb", [128, 64], F32))
        cw_p = e(nc.sbuf_tensor("cw_pb", [128, 48], F32))

        for name, buf in [("ident", ident), ("bu_p", bu_p), ("bv_p", bv_p),
                          ("bg1_p", bg1_p), ("bg2_p", bg2_p), ("cb_p", cb_p),
                          ("db_p", db_p), ("ub_p", ub_p), ("cw_p", cw_p)]:
            S("sync", lambda e_, a=d_in[name], b=buf: e_.dma_start(out=b[:], in_=a),
              inc=tS, n=16)
        small_ld = aft(tS)

        S("gpsimd", lambda e_: e_.dma_start(out=d_out, in_=d_in["xT"]), inc=tG, n=16)
        prefill_done = aft(tG)

        with contextlib.ExitStack() as p0:
            e0 = p0.enter_context
            cvt = [e0(nc.sbuf_tensor(f"cvt{i}", [128, 2048], BF16)) for i in range(2)]
            xln = [e0(nc.sbuf_tensor(f"xln{i}", [128, 2048], F32)) for i in range(2)]
            xsq = [e0(nc.sbuf_tensor(f"xsq{i}", [128, 2048], F32)) for i in range(2)]
            xnbf = [e0(nc.sbuf_tensor(f"xnbf{i}", [128, 2048], BF16)) for i in range(2)]
            stg = [e0(nc.sbuf_tensor(f"stg{i}", [128, 2048], BF16)) for i in range(2)]
            ls1 = e0(nc.sbuf_tensor("ls1", [128, 1], F32))
            ls2 = e0(nc.sbuf_tensor("ls2", [128, 1], F32))
            lm = e0(nc.sbuf_tensor("lm", [128, 1], F32))
            lv = e0(nc.sbuf_tensor("lv", [128, 1], F32))
            lsd = e0(nc.sbuf_tensor("lsd", [128, 1], F32))
            lrs = e0(nc.sbuf_tensor("lrs", [128, 1], F32))

            cvt_st = {}
            blocks = [(d_in["W2u"], w2u_scr, bt) for bt in range(16)] + \
                     [(d_in["W2v"], w2v_scr, bt) for bt in range(16)]
            for i, (srct, dst, bt) in enumerate(blocks):
                sl = i % 2
                w = [cvt_st[sl]] if sl in cvt_st else []
                S("gpsimd", lambda e_, a=srct, b=cvt[sl], t=bt: e_.dma_start(
                    out=b[:], in_=a[t].rearrange("p a b -> p (a b)")), waits=w, inc=tCV[sl], n=16)
                S("sync", lambda e_, a=cvt[sl], b=dst, t=bt: e_.dma_start(
                    out=b[t], in_=a[:]), waits=[aft(tCV[sl])],
                  inc=tCS[sl], n=16)
                cvt_st[sl] = aft(tCS[sl])
            wcvt_done_S = aft(tCS[0])
            wcvt_done_S2 = aft(tCS[1])

            xld = {}
            xn_done = {}
            tr_done = {}
            ev_done = {}
            st_done = {}
            sqd = {}
            sqrt_done = {}
            for lt in range(32):
                sl = lt % 2
                w = [st_done[lt - 2]] if lt - 2 in st_done else []
                if lt - 2 in xn_done:
                    w.append(xn_done[lt - 2])
                if lt - 2 in sqd:
                    w.append(sqd[lt - 2])
                S("sync", lambda e_, b=xln[sl], t=lt: e_.dma_start(
                    out=b[:], in_=d_in["x_tok"][t * 128:(t + 1) * 128, :]),
                  waits=w, inc=tS, n=16)
                xld[lt] = aft(tS)
                S("vector", lambda e_, b=xln[sl]: e_.reduce_sum(ls1[:], b[:], AX.X),
                  waits=[xld[lt]])
                S("scalar", lambda e_, b=xln[sl], o=xsq[sl]: e_.activation(
                    o[:], b[:], AF.Square),
                  waits=[xld[lt]] + ([xn_done[lt - 2]] if lt - 2 in xn_done else []),
                  inc=tSc)
                sqd[lt] = aft(tSc)
                S("vector", lambda e_, b=xsq[sl]: e_.reduce_sum(ls2[:], b[:], AX.X),
                  waits=[sqd[lt]])
                S("vector", lambda e_: e_.tensor_single_scalar(lm[:], ls1[:], 1.0 / D, ALU.mult))
                S("vector", lambda e_: e_.tensor_scalar(lv[:], ls2[:], 1.0 / D, EPS, ALU.mult, ALU.add),
                  waits=([sqrt_done[lt - 1]] if lt - 1 in sqrt_done else []))
                S("vector", lambda e_: e_.tensor_tensor(ls1[:], lm[:], lm[:], ALU.mult))
                S("vector", lambda e_: e_.tensor_tensor(lv[:], lv[:], ls1[:], ALU.subtract),
                  inc=tV)
                var_done = aft(tV)
                S("scalar", lambda e_: e_.activation(lsd[:], lv[:], AF.Sqrt),
                  waits=[var_done], inc=tSc)
                sqrt_done[lt] = aft(tSc)
                S("vector", lambda e_: e_.reciprocal(lrs[:], lsd[:]), waits=[sqrt_done[lt]])
                S("vector", lambda e_, b=xln[sl], o=xnbf[sl]: e_.tensor_scalar(
                    o[:], b[:], lm[:, 0:1], lrs[:, 0:1], ALU.subtract, ALU.mult),
                  waits=([tr_done[(lt - 2, 3)]] if (lt - 2, 3) in tr_done else []), inc=tV)
                xn_done[lt] = aft(tV)
                pHb = pH[:].bitcast(BF16)
                for gq in range(4):
                    wq = [xn_done[lt]]
                    key = (lt, gq)
                    pkey = (lt, gq - 2) if gq >= 2 else (lt - 1, gq + 2)
                    if pkey in ev_done:
                        wq.append(ev_done[pkey])
                    for j in range(4):
                        dtt = gq * 4 + j
                        S("tensor", lambda e_, src=xnbf[sl], o=pHb, g=gq, jj=j, d2=dtt: e_.transpose(
                            o[:, (g % 2) * 1024 + jj * 128:(g % 2) * 1024 + jj * 128 + 128],
                            src[:, d2 * 128:(d2 + 1) * 128], ident[:]),
                          waits=(wq if j == 0 else []), inc=(tT if j == 3 else None))
                    tr_done[key] = aft(tT)
                    S("scalar", lambda e_, o=stg[sl], g=gq, ph=pHb: e_.copy(
                        o[:, g * 512:(g + 1) * 512], ph[:, (g % 2) * 1024:(g % 2) * 1024 + 512]),
                      waits=[tr_done[key]], inc=tSc)
                    ev_done[key] = aft(tSc)
                S("sync", lambda e_, b=stg[sl], t=lt: e_.dma_start(
                    out=xnT_scr[t].rearrange("a p c -> p a c"),
                    in_=b[:].rearrange("p (a c) -> p a c", c=128)),
                  waits=[ev_done[(lt, 3)]], inc=tS, n=16)
                st_done[lt] = aft(tS)
            xnT_all_stored = aft(tS)

            with contextlib.ExitStack() as q0:
                eq = q0.enter_context
                qbuf = {}
                for en in ("vector", "gpsimd"):
                    qbuf[en] = dict(
                        qin=[eq(nc.sbuf_tensor(f"qin_{en}{i}", [128, 512], F32)) for i in range(2)],
                        z=eq(nc.sbuf_tensor(f"z_{en}", [128, 512], F32)),
                        za=eq(nc.sbuf_tensor(f"za_{en}", [128, 512], F32)),
                        cnt=[eq(nc.sbuf_tensor(f"cnt_{en}{i}", [128, 512], F32)) for i in range(2)],
                        b1=eq(nc.sbuf_tensor(f"b1_{en}", [128, 512], F32)),
                        b2=eq(nc.sbuf_tensor(f"b2_{en}", [128, 512], F32)),
                        b3=eq(nc.sbuf_tensor(f"b3_{en}", [128, 512], F32)),
                        sg=eq(nc.sbuf_tensor(f"sg_{en}", [128, 512], I32)),
                        qo=[eq(nc.sbuf_tensor(f"qo_{en}{i}", [128, 512], BF16)) for i in range(2)],
                        am=eq(nc.sbuf_tensor(f"am_{en}", [128, 16], F32)),
                        eb=eq(nc.sbuf_tensor(f"eb_{en}", [128, 16], I32)),
                        isc=eq(nc.sbuf_tensor(f"isc_{en}", [128, 16], F32)),
                        scl=eq(nc.sbuf_tensor(f"scl_{en}", [128, 16], F32)),
                    )
                qstg = [eq(nc.sbuf_tensor(f"qstg{i}", [128, 512], BF16)) for i in range(2)]

                tiles = []
                for pt in range(8):
                    for dc in range(4):
                        tiles.append(("up", pt, dc))
                for pt in range(2):
                    for fc in range(16):
                        tiles.append(("dn", pt, fc))

                qld = {}
                qdone = {}
                trq_done = {}
                evq_done = {}
                stq_done = {}
                eng_slot = {"vector": 0, "gpsimd": 0}
                for ti, (kind, pt, cc) in enumerate(tiles):
                    en = "vector"
                    tE = tV if en == "vector" else tG2
                    bb = qbuf[en]
                    sl = eng_slot[en] % 2
                    eng_slot[en] += 1
                    if kind == "up":
                        src = d_in["upT_sl"][pt * 128:(pt + 1) * 128, cc * 512:(cc + 1) * 512]
                    else:
                        src = d_in["dnT_sl"][pt * 128:(pt + 1) * 128, cc * 512:(cc + 1) * 512]
                    w = []
                    prev_same = [k for k in range(ti) if (("vector" if k % 3 != 2 else "gpsimd") == en)]
                    if len(prev_same) >= 2:
                        w = [qdone[prev_same[-2]]]
                    S("sync", lambda e_, a=src, b=bb["qin"][sl]: e_.dma_start(out=b[:], in_=a),
                      waits=w, inc=tS, n=16)
                    qld[ti] = aft(tS)
                    qin, z, za, cnt = bb["qin"][sl], bb["z"], bb["za"], bb["cnt"]
                    b1, b2, b3, sg, qo = bb["b1"], bb["b2"], bb["b3"], bb["sg"], bb["qo"][sl]
                    am, eb, isc, scl = bb["am"], bb["eb"], bb["isc"], bb["scl"]
                    q3 = lambda t: t[:].rearrange("p (b k) -> p b k", k=32)
                    wr = [qld[ti]]
                    if en != "vector" and prev_same:
                        wr.append(qdone[prev_same[-1]])
                    S("vector", lambda e_, q=qin, o=am: e_.tensor_reduce(
                        o[:], q3(q), AX.X, ALU.max, apply_absolute_value=True),
                      waits=wr)
                    S("vector", lambda e_, o=am: e_.tensor_single_scalar(o[:], o[:], 1e-30, ALU.max))
                    S("vector", lambda e_, a=am, o=eb: e_.tensor_single_scalar(
                        o[:], a[:].bitcast(I32), EXP_MASK, ALU.bitwise_and))
                    S("vector", lambda e_, a=eb, o=scl: e_.tensor_single_scalar(
                        o[:], a[:].bitcast(F32), 0.25, ALU.mult))
                    S("vector", lambda e_, a=scl, o=isc: e_.reciprocal(o[:], a[:]))
                    S("vector", lambda e_, q=qin, o=sg: e_.tensor_single_scalar(
                        o[:], q[:].bitcast(I32), SGN_MASK, ALU.bitwise_and), inc=tV)
                    scales_done = aft(tV)
                    S(en, lambda e_, q=qin, a=isc, o=z: e_.tensor_tensor(
                        q3(o), q3(q), a[:].broadcast_to([128, 16, 32]), ALU.mult),
                      waits=([scales_done] if en != "vector" else []), inc=tE)
                    z_done = aft(tE)
                    S("scalar", lambda e_, a=z, o=za: e_.activation(o[:], a[:], AF.Abs),
                      waits=[z_done], inc=tSc)
                    za_done = aft(tSc)
                    S(en, lambda e_, a=za, o=cnt[0]: e_.tensor_single_scalar(
                        o[:], a[:], MIDS[0], ALU.is_ge), waits=[za_done])
                    for mi, mid in enumerate(MIDS[1:]):
                        src_c, dst_c = cnt[mi % 2], cnt[(mi + 1) % 2]
                        S(en, lambda e_, a=src_c, b=dst_c, m=mid, x=za: e_.scalar_tensor_tensor(
                            b[:], x[:], m, a[:], ALU.is_ge, ALU.add))
                    cfin = cnt[(len(MIDS) - 1) % 2]
                    S(en, lambda e_, c=cfin, o=b1: e_.tensor_single_scalar(o[:], c[:], 0.5, ALU.mult))
                    S(en, lambda e_, c=cfin, o=b2: e_.tensor_single_scalar(o[:], c[:], -2.0, ALU.add))
                    S(en, lambda e_, a=b1, b=b2, o=b3: e_.tensor_tensor(o[:], a[:], b[:], ALU.max))
                    S(en, lambda e_, c=cfin, o=b1: e_.tensor_scalar(
                        o[:], c[:], 2.0, -8.0, ALU.mult, ALU.add))
                    S(en, lambda e_, a=b3, b=b1, o=b2: e_.tensor_tensor(o[:], a[:], b[:], ALU.max))
                    S(en, lambda e_, a=b2, o=b1: e_.tensor_single_scalar(o[:], a[:], 6.0, ALU.min))
                    S(en, lambda e_, a=b1, s2=scl, o=b2: e_.tensor_tensor(
                        o[:].rearrange("p (b k) -> p b k", k=32),
                        a[:].rearrange("p (b k) -> p b k", k=32),
                        s2[:].broadcast_to([128, 16, 32]), ALU.mult), inc=tE)
                    deq_done = aft(tE)
                    S("vector", lambda e_, a=b2, b=sg, o=b3: e_.tensor_tensor(
                        o[:].bitcast(I32), a[:].bitcast(I32), b[:], ALU.bitwise_or),
                      waits=([deq_done] if en != "vector" else []))
                    S("vector", lambda e_, a=b3, o=qo: e_.tensor_copy(o[:], a[:]), inc=tV)
                    qdone[ti] = aft(tV)
                    pq = pU[ti % 2][:].bitcast(BF16)
                    wq = [qdone[ti]]
                    if ti - 2 in evq_done:
                        wq.append(evq_done[ti - 2])
                    for j in range(4):
                        S("tensor", lambda e_, o=pq, src=qo, jj=j: e_.transpose(
                            o[:, jj * 128:(jj + 1) * 128], src[:, jj * 128:(jj + 1) * 128], ident[:]),
                          waits=(wq if j == 0 else []), inc=(tT if j == 3 else None))
                    trq_done[ti] = aft(tT)
                    wq = [trq_done[ti]]
                    if ti - 2 in stq_done:
                        wq.append(stq_done[ti - 2])
                    S("scalar", lambda e_, o=qstg[ti % 2], p=pq: e_.copy(o[:], p[:, 0:512]),
                      waits=wq, inc=tSc)
                    evq_done[ti] = aft(tSc)
                    if kind == "up":
                        dst = q_up_in[pt * 128:(pt + 1) * 128, cc * 512:(cc + 1) * 512]
                    else:
                        dst = q_dn_in[pt * 128:(pt + 1) * 128, cc * 512:(cc + 1) * 512]
                    S("sync", lambda e_, b=qstg[ti % 2], d2=dst: e_.dma_start(
                        out=d2.rearrange("p (j f) -> p j f", f=128),
                        in_=b[:].rearrange("p (j f) -> p j f", f=128)),
                      waits=[evq_done[ti]], inc=tQS, n=16)
                    stq_done[ti] = aft(tQS)
                q_in_stored = aft(tQS)


        p0S, p0T, p0V, p0Sc, p0G, p0G2 = aft(tS), aft(tT), aft(tV), aft(tSc), aft(tG), aft(tG2)
        fence0 = [p0S, p0T, p0V, p0Sc, p0G, p0G2, aft(tQS), aft(tCS[0]), aft(tCS[1])]

        with contextlib.ExitStack() as pr:
            er = pr.enter_context
            Wg1T_sb = er(nc.sbuf_tensor("Wg1T_sb", [128, 16, 512], BF16))
            Wg2T_sb = er(nc.sbuf_tensor("Wg2T_sb", [128, 4, 2048], BF16))
            diag_sb = er(nc.sbuf_tensor("diag_sb", [128, 48, 128], BF16))
            xnT_sb = er(nc.sbuf_tensor("xnT_sb", [128, 16, 768], BF16))
            hT_sb = er(nc.sbuf_tensor("hT_sb", [128, 4, 512], BF16))
            sigT = [er(nc.sbuf_tensor(f"sigT{i}", [128, 512], F32)) for i in range(2)]
            ugT = [er(nc.sbuf_tensor(f"ugT{i}", [128, 512], F32)) for i in range(2)]
            kcT = [er(nc.sbuf_tensor(f"kcT{i}", [128, 512], F32)) for i in range(2)]
            vgT = [er(nc.sbuf_tensor(f"vgT{i}", [128, 512], BF16)) for i in range(2)]
            gvT = er(nc.sbuf_tensor("gvT", [128, 16, 512], BF16))
            s_stage = er(nc.sbuf_tensor("s_stage", [128, 16, 512], F32))
            wu_ring = er(nc.sbuf_tensor("wu_ring", [128, 3, 16, 128], BF16))
            wv_ring = er(nc.sbuf_tensor("wv_ring", [128, 3, 16, 128], BF16))
            wz_ring = er(nc.sbuf_tensor("wz_ring", [128, 3, 16, 128], BF16))
            carry = er(nc.sbuf_tensor("carry", [128, 16], F32))

            S("gpsimd", lambda e_: e_.dma_start(
                out=Wg1T_sb[:], in_=d_in["Wg1T"].rearrange("(b p) h -> p b h", p=128)),
              waits=fence0, inc=tG, n=16)
            S("gpsimd", lambda e_: e_.dma_start(
                out=Wg2T_sb[:], in_=d_in["Wg2T"].rearrange("(b p) r -> p b r", p=128)),
              inc=tG, n=16)
            wg_ld = aft(tG)
            S("gpsimd", lambda e_: e_.collective_compute(
                "AllGather", ALU.bypass, replica_groups=[list(range(NCORES))],
                ins=[q_up_in.opt()], outs=[q_up.opt()]),
              waits=[q_in_stored], inc=tC)
            S("gpsimd", lambda e_: e_.collective_compute(
                "AllGather", ALU.bypass, replica_groups=[list(range(NCORES))],
                ins=[q_dn_in.opt()], outs=[q_dn.opt()]),
              inc=tC)
            cc_done = aft(tC)
            S("vector", lambda e_: e_.memset(carry[:], 0.0), waits=fence0, inc=tV)
            for k in range(3):
                for dtt in range(16):
                    S("vector", lambda e_, kk=k, d2=dtt: e_.tensor_scalar(
                        diag_sb[:, kk * 16 + d2, :], ident[:], cw_p[:, kk * 16 + d2:kk * 16 + d2 + 1],
                        None, ALU.mult), waits=[small_ld], inc=(tV if (k, dtt) == (2, 15) else None))
            diag_done = aft(tV)

            scr_r = xnT_scr.rearrange("a b p c -> p b a c")

            wu_ld, wv_ld, wz_ld = {}, {}, {}
            u_done, v_done, g_done = {}, {}, {}
            sig_done, ug_done, vg_done, gv_done = {}, {}, {}, {}
            uv_done, scp_done = {}, {}
            relu_done = {}
            s_store = {}
            xnT_release = (tT, 0)
            rt_end = {}
            uv_end_prev = (tT, 0)
            g_mms_prev = (tT, 0)
            memset_done = None

            def wu_dma(ls, rt, extra=()):
                slot = rt % 3
                S("sync", lambda e_, r2=rt, s2=slot: e_.dma_start(
                    out=wu_ring[:, s2, :, :].rearrange("p a b -> p (a b)"), in_=w2u_scr[r2]),
                  waits=list(extra), inc=tWU[slot], n=16)
                wu_ld[(ls, rt)] = aft(tWU[slot])
                S("sync", lambda e_, r2=rt, s2=slot: e_.dma_start(
                    out=wv_ring[:, s2, :, :].rearrange("p a b -> p (a b)"), in_=w2v_scr[r2]),
                  inc=tWV[slot], n=16)
                wv_ld[(ls, rt)] = aft(tWV[slot])

            def wz_dma(ls, dtt, extra=()):
                slot = dtt % 3
                S("sync", lambda e_, d2=dtt, s2=slot: e_.dma_start(
                    out=wz_ring[:, s2, :, :].rearrange("p a b -> p (a b)"), in_=w2u_scr[d2]),
                  waits=list(extra), inc=tWZ[slot], n=16)
                wz_ld[(ls, dtt)] = aft(tWZ[slot])

            for ls in range(NLS):
                lt0 = ls * 4
                w = [xnT_release]
                if ls == 0:
                    w.extend(fence0)
                for i in range(4):
                    S("sync", lambda e_, t=lt0 + i, o=128 + i * 128: e_.dma_start(
                        out=xnT_sb[:, :, o:o + 128], in_=scr_r[:, :, t, :]),
                      waits=(w if i == 0 else []), inc=tXN, n=16)
                if ls > 0:
                    S("sync", lambda e_, t=lt0 - 1: e_.dma_start(
                        out=xnT_sb[:, :, 0:128], in_=scr_r[:, :, t, :]), inc=tXN, n=16)
                else:
                    S("vector", lambda e_: e_.memset(xnT_sb[:, :, 127:128], 0.0),
                      waits=[xnT_release], inc=tV)
                    memset_done = aft(tV)
                if ls < NLS - 1:
                    S("sync", lambda e_, t=lt0 + 4: e_.dma_start(
                        out=xnT_sb[:, :, 640:768], in_=scr_r[:, :, t, :]), inc=tXN, n=16)
                else:
                    S("vector", lambda e_: e_.memset(xnT_sb[:, :, 640:641], 0.0),
                      waits=[xnT_release], inc=tV)
                    memset_done = aft(tV)
                xnT_ld = aft(tXN)

                prev_user = {0: 15, 1: 13, 2: 14}
                if ls == 0:
                    for rt in range(3):
                        wu_dma(ls, rt, extra=[wcvt_done_S, wcvt_done_S2] + fence0)
                    for dtt in range(3):
                        wz_dma(ls, dtt, extra=([wcvt_done_S, wcvt_done_S2] + fence0 if dtt == 0 else []))
                else:
                    for rt in range(3):
                        wu_dma(ls, rt, extra=[rt_end[(ls - 1, prev_user[rt])]])
                    for dtt in range(3):
                        wz_dma(ls, dtt, extra=[uv_done[(ls - 1, prev_user[dtt])]])

                for ht in range(4):
                    half = (ht % 2)
                    w = [xnT_ld]
                    if ls == 0 and ht == 0:
                        w += [wg_ld, p0V, p0Sc, p0G2, diag_done]
                    if (ls, ht - 2) in relu_done:
                        w.append(relu_done[(ls, ht - 2)])
                    if ls > 0 and ht < 2:
                        w.append(relu_done[(ls - 1, ht + 2)])
                    for dtt in range(16):
                        S("tensor", lambda e_, d2=dtt, h2=ht, hf=half: e_.matmul(
                            pH[:, hf * 512:hf * 512 + 512],
                            Wg1T_sb[:, d2, h2 * 128:(h2 + 1) * 128],
                            xnT_sb[:, d2, 128:640],
                            start=(d2 == 0), stop=(d2 == 15)),
                          waits=(w if dtt == 0 else []), inc=(tT if dtt == 15 else None))
                    h_mm = aft(tT)
                    w2 = [h_mm, small_ld]
                    w2.append(g_mms_prev)
                    S("scalar", lambda e_, h2=ht, hf=half: e_.activation(
                        hT_sb[:, h2, :], pH[:, hf * 512:hf * 512 + 512], AF.Relu,
                        bias=bg1_p[:, h2:h2 + 1]), waits=w2, inc=tSc)
                    relu_done[(ls, ht)] = aft(tSc)

                for rt in range(16):
                    par = rt % 2
                    w = [wu_ld[(ls, rt)], xnT_ld]
                    if (ls, rt - 2) in ug_done:
                        w.append(ug_done[(ls, rt - 2)])
                    elif ls > 0 and rt < 2:
                        w.append(ug_done[(ls - 1, rt + 14)])
                    for dtt in range(16):
                        S("tensor", lambda e_, d2=dtt, r2=rt: e_.matmul(
                            pU[r2 % 2][:], wu_ring[:, r2 % 3, d2, :], xnT_sb[:, d2, 128:640],
                            start=(d2 == 0), stop=(d2 == 15)),
                          waits=(w if dtt == 0 else []), inc=(tT if dtt == 15 else None))
                    u_done[(ls, rt)] = aft(tT)
                    w = [wv_ld[(ls, rt)]]
                    if (ls, rt - 2) in vg_done:
                        w.append(vg_done[(ls, rt - 2)])
                    elif ls > 0 and rt < 2:
                        w.append(vg_done[(ls - 1, rt + 14)])
                    for dtt in range(16):
                        S("tensor", lambda e_, d2=dtt, r2=rt: e_.matmul(
                            pV[r2 % 2][:], wv_ring[:, r2 % 3, d2, :], xnT_sb[:, d2, 128:640],
                            start=(d2 == 0), stop=(d2 == 15)),
                          waits=(w if dtt == 0 else []), inc=(tT if dtt == 15 else None))
                    v_done[(ls, rt)] = aft(tT)
                    w = [relu_done[(ls, 3)]]
                    if (ls, rt - 2) in sig_done:
                        w.append(sig_done[(ls, rt - 2)])
                    elif ls > 0 and rt < 2:
                        w.append(sig_done[(ls - 1, rt + 14)])
                    for ht in range(4):
                        S("tensor", lambda e_, h2=ht, r2=rt: e_.matmul(
                            pG[r2 % 2][:], Wg2T_sb[:, h2, r2 * 128:(r2 + 1) * 128], hT_sb[:, h2, :],
                            start=(h2 == 0), stop=(h2 == 3)),
                          waits=(w if ht == 0 else []), inc=(tT if ht == 3 else None))
                    g_done[(ls, rt)] = aft(tT)
                    if rt + 3 < 16:
                        wu_dma(ls, rt + 3, extra=[u_done[(ls, rt)], v_done[(ls, rt)]])
                    w = [g_done[(ls, rt)]]
                    if (ls, rt - 2) in vg_done:
                        w.append(vg_done[(ls, rt - 2)])
                    elif ls > 0 and rt < 2:
                        w.append(vg_done[(ls - 1, rt + 14)])
                    S("scalar", lambda e_, r2=rt: e_.activation(
                        sigT[r2 % 2][:], pG[r2 % 2][:], AF.Sigmoid, bias=bg2_p[:, r2:r2 + 1]),
                      waits=w, inc=tSc)
                    sig_done[(ls, rt)] = aft(tSc)
                    S("vector", lambda e_, r2=rt: e_.scalar_tensor_tensor(
                        ugT[r2 % 2][:], pU[r2 % 2][:], bu_p[:, r2:r2 + 1], sigT[r2 % 2][:],
                        ALU.add, ALU.mult),
                      waits=[u_done[(ls, rt)], sig_done[(ls, rt)]], inc=tV)
                    ug_done[(ls, rt)] = aft(tV)
                    for rep in range(2):
                        S("vector", lambda e_, r2=rt: e_.tensor_tensor_scan(
                            kcT[r2 % 2][:], ugT[r2 % 2][:], ugT[r2 % 2][:], carry[:, r2:r2 + 1],
                            ALU.add, ALU.bypass))
                    S("vector", lambda e_, r2=rt: e_.tensor_copy(
                        carry[:, r2:r2 + 1], kcT[r2 % 2][:, 511:512]))
                    S("vector", lambda e_, r2=rt: e_.scalar_tensor_tensor(
                        vgT[r2 % 2][:], pV[r2 % 2][:], bv_p[:, r2:r2 + 1], sigT[r2 % 2][:],
                        ALU.add, ALU.mult),
                      waits=[v_done[(ls, rt)]], inc=tV)
                    vg_done[(ls, rt)] = aft(tV)
                    wgv = []
                    if rt == 0:
                        wgv.append(uv_end_prev)
                    S("vector", lambda e_, r2=rt: e_.tensor_tensor(
                        gvT[:, r2, :], kcT[r2 % 2][:], vgT[r2 % 2][:], ALU.mult),
                      waits=wgv, inc=tV)
                    gv_done[(ls, rt)] = aft(tV)
                    rt_end[(ls, rt)] = aft(tT)
                g_mms_prev = aft(tT)

                for dtt in range(16):
                    par = dtt % 2
                    w0 = [wz_ld[(ls, dtt)], vg_done[(ls, 14 + par)]]
                    if (ls, dtt - 2) in scp_done:
                        w0.append(scp_done[(ls, dtt - 2)])
                    if memset_done is not None:
                        w0.append(memset_done)
                    for rt in range(16):
                        w = list(w0) if rt == 0 else [gv_done[(ls, rt)]]
                        if rt == 0:
                            w.append(gv_done[(ls, 0)])
                        S("tensor", lambda e_, d2=dtt, r2=rt: e_.matmul(
                            pV[d2 % 2][:], wz_ring[:, d2 % 3, r2, :], gvT[:, r2, :],
                            start=(r2 == 0), stop=False),
                          waits=w, inc=None)
                    for k in range(3):
                        off = 127 + k
                        S("tensor", lambda e_, d2=dtt, kk=k, o=off: e_.matmul(
                            pV[d2 % 2][:], diag_sb[:, kk * 16 + d2, :], xnT_sb[:, d2, o:o + 512],
                            start=False, stop=(kk == 2)),
                          inc=(tT if k == 2 else None))
                    uv_done[(ls, dtt)] = aft(tT)
                    if dtt + 3 < 16:
                        wz_dma(ls, dtt + 3, extra=[uv_done[(ls, dtt)]])
                    w = [uv_done[(ls, dtt)]]
                    if ls - 1 in s_store:
                        w.append(s_store[ls - 1])
                    S("scalar", lambda e_, d2=dtt: e_.activation(
                        s_stage[:, d2, :], pV[d2 % 2][:], AF.Identity, bias=cb_p[:, d2:d2 + 1]),
                      waits=w, inc=tSc)
                    scp_done[(ls, dtt)] = aft(tSc)
                uv_end_prev = aft(tT)
                xnT_release = aft(tT)

                S("sync", lambda e_, l2=ls: e_.dma_start(
                    out=s_scr[:, :, l2 * 512:(l2 + 1) * 512], in_=s_stage[:]),
                  waits=[scp_done[(ls, 15)]], inc=tS, n=16)
                s_store[ls] = aft(tS)

        prS, prT, prV, prSc, prG = aft(tS), aft(tT), aft(tV), aft(tSc), aft(tG)
        fenceR = [prS, prT, prV, prSc, prG, aft(tXN)] + \
            [aft(t) for t in tWU + tWV + tWZ]

        with contextlib.ExitStack() as pf:
            ef = pf.enter_context
            xnF = ef(nc.sbuf_tensor("xnF", [128, 16, 512], BF16))
            sF = ef(nc.sbuf_tensor("sF", [128, 16, 512], F32))
            aT = ef(nc.sbuf_tensor("aT", [128, 64, 512], BF16))
            up_ring = ef(nc.sbuf_tensor("up_ring", [128, 3, 16, 128], BF16))
            dn_ring = ef(nc.sbuf_tensor("dn_ring", [128, 2, 64, 128], BF16))
            xgf = [ef(nc.sbuf_tensor(f"xgf{i}", [128, 512], F32)) for i in range(2)]
            gA = [ef(nc.sbuf_tensor(f"gA{i}", [128, 512], F32)) for i in range(2)]
            gB = [ef(nc.sbuf_tensor(f"gB{i}", [128, 512], F32)) for i in range(2)]
            gt = [ef(nc.sbuf_tensor(f"gt{i}", [128, 512], F32)) for i in range(2)]
            gt2 = [ef(nc.sbuf_tensor(f"gt2{i}", [128, 512], F32)) for i in range(2)]

            up_ld, dn_ld = {}, {}
            upm_done, a_done = {}, {}
            dnm_done, silu_done, sg_done = {}, {}, {}
            acc_done = {}
            xn_ld, s_ld = {}, {}
            up_phase_end = (tT, 0)
            dn_phase_end = (tT, 0)

            def up_dma(ls, ft, extra=()):
                S("sync", lambda e_, f2=ft: e_.dma_start(
                    out=up_ring[:, f2 % 3, :, :].rearrange("p a b -> p (a b)"),
                    in_=q_up[f2 * 128:(f2 + 1) * 128, :]),
                  waits=list(extra), inc=tUPs[ft % 3], n=16)
                up_ld[(ls, ft)] = aft(tUPs[ft % 3])

            def dn_dma(ls, dtt, extra=()):
                S("sync", lambda e_, d2=dtt: e_.dma_start(
                    out=dn_ring[:, d2 % 2, :, :].rearrange("p a b -> p (a b)"),
                    in_=q_dn[d2 * 128:(d2 + 1) * 128, :]),
                  waits=list(extra), inc=tDNs[dtt % 2], n=16)
                dn_ld[(ls, dtt)] = aft(tDNs[dtt % 2])

            for ls in range(NLS):
                lt0 = ls * 4
                w = [up_phase_end]
                if ls == 0:
                    w += fenceR + [cc_done]
                for i in range(4):
                    S("sync", lambda e_, t=lt0 + i, o=i * 128: e_.dma_start(
                        out=xnF[:, :, o:o + 128], in_=scr_r[:, :, t, :]),
                      waits=(w if i == 0 else []), inc=tXF, n=16)
                xn_ld[ls] = aft(tXF)
                w = []
                if ls - 1 in acc_done:
                    w.append(acc_done[ls - 1])
                if ls == 0:
                    w += fenceR
                S("sync", lambda e_, l2=ls: e_.dma_start(
                    out=sF[:], in_=s_scr[:, :, l2 * 512:(l2 + 1) * 512]),
                  waits=w, inc=tSF, n=16)
                s_ld[ls] = aft(tSF)

                prev_user_up = {0: 63, 1: 61, 2: 62}
                if ls == 0:
                    for ft in range(3):
                        up_dma(ls, ft, extra=([cc_done] + fenceR if ft == 0 else []))
                else:
                    for ft in range(3):
                        up_dma(ls, ft, extra=[upm_done[(ls - 1, prev_user_up[ft])]])

                for ft in range(64):
                    par = ft % 2
                    w = [up_ld[(ls, ft)], xn_ld[ls]]
                    if (ls, ft - 2) in a_done:
                        w.append(a_done[(ls, ft - 2)])
                    elif ls > 0 and ft < 2:
                        w.append(a_done[(ls - 1, ft + 62)])
                    for dtt in range(16):
                        S("tensor", lambda e_, d2=dtt, f2=ft: e_.matmul(
                            pU[f2 % 2][:], up_ring[:, f2 % 3, d2, :], xnF[:, d2, :],
                            start=(d2 == 0), stop=(d2 == 15)),
                          waits=(w if dtt == 0 else []), inc=(tT if dtt == 15 else None))
                    upm_done[(ls, ft)] = aft(tT)
                    if ft + 3 < 64:
                        up_dma(ls, ft + 3, extra=[upm_done[(ls, ft)]])
                    w = [upm_done[(ls, ft)]]
                    if (ls, ft - 2) in a_done:
                        w.append(a_done[(ls, ft - 2)])
                    elif ls > 0 and ft < 2:
                        w.append(a_done[(ls - 1, ft + 62)])
                    S("scalar", lambda e_, f2=ft: e_.activation(
                        xgf[f2 % 2][:], pU[f2 % 2][:], AF.Identity, bias=ub_p[:, f2:f2 + 1]),
                      waits=w, inc=tSc)
                    xg_done = aft(tSc)
                    S("vector", lambda e_, f2=ft: e_.tensor_tensor(
                        gA[f2 % 2][:], xgf[f2 % 2][:], xgf[f2 % 2][:], ALU.mult),
                      waits=[xg_done])
                    S("vector", lambda e_, f2=ft: e_.tensor_scalar(
                        gB[f2 % 2][:], gA[f2 % 2][:], GELU_C * GELU_S, GELU_S, ALU.mult, ALU.add))
                    S("vector", lambda e_, f2=ft: e_.tensor_tensor(
                        gA[f2 % 2][:], gB[f2 % 2][:], xgf[f2 % 2][:], ALU.mult), inc=tV)
                    u_in_done = aft(tV)
                    S("scalar", lambda e_, f2=ft: e_.activation(
                        gB[f2 % 2][:], gA[f2 % 2][:], AF.Sigmoid),
                      waits=[u_in_done], inc=tSc)
                    sg_act = aft(tSc)
                    wv2 = [sg_act]
                    if ft == 0:
                        wv2.append(dn_phase_end)
                    S("vector", lambda e_, f2=ft: e_.tensor_tensor(
                        aT[:, f2, :], xgf[f2 % 2][:], gB[f2 % 2][:], ALU.mult),
                      waits=wv2, inc=tV)
                    a_done[(ls, ft)] = aft(tV)
                up_phase_end = aft(tT)

                if ls == 0:
                    dn_dma(ls, 0, extra=[cc_done] + fenceR)
                    dn_dma(ls, 1)
                else:
                    dn_dma(ls, 0, extra=[dnm_done[(ls - 1, 14)]])
                    dn_dma(ls, 1, extra=[dnm_done[(ls - 1, 15)]])
                for dtt in range(16):
                    par = dtt % 2
                    w0 = [dn_ld[(ls, dtt)]]
                    if (ls, dtt - 2) in sg_done:
                        w0.append(sg_done[(ls, dtt - 2)])
                    elif ls > 0 and dtt < 2:
                        w0.append(sg_done[(ls - 1, dtt + 14)])
                    for ft in range(64):
                        w = (w0 + [a_done[(ls, ft)]]) if ft == 0 else [a_done[(ls, ft)]]
                        S("tensor", lambda e_, d2=dtt, f2=ft: e_.matmul(
                            pV[d2 % 2][:], dn_ring[:, d2 % 2, f2, :], aT[:, f2, :],
                            start=(f2 == 0), stop=(f2 == 63)),
                          waits=w, inc=(tT if ft == 63 else None))
                    dnm_done[(ls, dtt)] = aft(tT)
                    if dtt + 2 < 16:
                        dn_dma(ls, dtt + 2, extra=[dnm_done[(ls, dtt)]])
                    w = [dnm_done[(ls, dtt)]]
                    if (ls, dtt - 2) in sg_done:
                        w.append(sg_done[(ls, dtt - 2)])
                    S("scalar", lambda e_, d2=dtt: e_.activation(
                        gt[d2 % 2][:], pV[d2 % 2][:], AF.Sigmoid, bias=db_p[:, d2:d2 + 1]),
                      waits=w, inc=tSc)
                    sig_dn = aft(tSc)
                    S("vector", lambda e_, d2=dtt: e_.scalar_tensor_tensor(
                        gt2[d2 % 2][:], pV[d2 % 2][:], db_p[:, d2:d2 + 1], gt[d2 % 2][:],
                        ALU.add, ALU.mult),
                      waits=[sig_dn, s_ld[ls]])
                    S("vector", lambda e_, d2=dtt: e_.tensor_tensor(
                        sF[:, d2, :], sF[:, d2, :], gt2[d2 % 2][:], ALU.mult), inc=tV)
                    sg_done[(ls, dtt)] = aft(tV)
                dn_phase_end = aft(tT)

                w = [sg_done[(ls, 15)], prefill_done]
                S("gpsimd", lambda e_, l2=ls: e_.dma_start(
                    out=d_out.rearrange("(b p) l -> p b l", p=128)[:, :, l2 * 512:(l2 + 1) * 512],
                    in_=sF[:], accum_op=ALU.add),
                  waits=w, inc=tACC, n=16)
                acc_done[ls] = aft(tACC)

        S("gpsimd", lambda e_: e_.engine_nop(), waits=[aft(tG), aft(tACC)])
        S("sync", lambda e_: e_.nop(), waits=[aft(tS), aft(tG), aft(tC), aft(tXF), aft(tSF)] +
          [aft(t) for t in tUPs + tDNs + tCV + tCS])

        block = e(nc.Block())

        def runner(name):
            def run(engine):
                last = {}
                for fn, ws, inc, n in sched.ops[name]:
                    for t, v in ws:
                        if last.get(t.name, -1) < v:
                            engine.wait_ge(t.h, v)
                            last[t.name] = v
                    ins = fn(engine)
                    if inc is not None:
                        ins.then_inc(inc.h, n)
            return run

        block.sync(runner("sync"))
        block.tensor(runner("tensor"))
        block.vector(runner("vector"))
        block.scalar(runner("scalar"))
        block.gpsimd(runner("gpsimd"))

    return nc


def kernel(**inputs):
    x = np.asarray(inputs["L0_attn_x"], dtype=np.float32)
    Wu = np.asarray(inputs["L0_attn_Wu"], dtype=np.float32)
    bu = np.asarray(inputs["L0_attn_bu"], dtype=np.float32)
    Wv = np.asarray(inputs["L0_attn_Wv"], dtype=np.float32)
    bv = np.asarray(inputs["L0_attn_bv"], dtype=np.float32)
    Wg1 = np.asarray(inputs["L0_attn_Wg1"], dtype=np.float32)
    bg1 = np.asarray(inputs["L0_attn_bg1"], dtype=np.float32)
    Wg2 = np.asarray(inputs["L0_attn_Wg2"], dtype=np.float32)
    bg2 = np.asarray(inputs["L0_attn_bg2"], dtype=np.float32)
    cw = np.asarray(inputs["L0_attn_conv_w"], dtype=np.float32)
    cb = np.asarray(inputs["L0_attn_conv_b"], dtype=np.float32)
    up = np.asarray(inputs["L0_ffn_up_kernel"], dtype=np.float32)
    ub = np.asarray(inputs["L0_ffn_up_bias"], dtype=np.float32)
    dn = np.asarray(inputs["L0_ffn_down_kernel"], dtype=np.float32)
    db = np.asarray(inputs["L0_ffn_down_bias"], dtype=np.float32)

    pcol = lambda v: np.ascontiguousarray(v.reshape(-1, 128).T)
    W2u = np.ascontiguousarray(Wu.T.reshape(16, 128, 16, 128).transpose(2, 1, 0, 3))
    W2v = np.ascontiguousarray(Wv.T.reshape(16, 128, 16, 128).transpose(2, 1, 0, 3))
    Wg1T = np.ascontiguousarray(Wg1.T)
    Wg2T = np.ascontiguousarray(Wg2.T)
    upT = np.ascontiguousarray(up.T)
    dnT = np.ascontiguousarray(dn.T)
    cw_p = np.empty((128, 48), dtype=np.float32)
    for k in range(3):
        for dtt in range(16):
            cw_p[:, k * 16 + dtt] = cw[dtt * 128:(dtt + 1) * 128, 0, k]
    ident = np.eye(128, dtype=ml_dtypes.bfloat16)

    shared = {
        "W2u": W2u, "W2v": W2v, "Wg1T": Wg1T, "Wg2T": Wg2T,
        "bu_p": pcol(bu), "bv_p": pcol(bv), "bg1_p": pcol(bg1), "bg2_p": pcol(bg2),
        "cb_p": pcol(cb), "db_p": pcol(db), "ub_p": pcol(ub), "cw_p": cw_p,
        "ident": ident,
    }
    in_maps = []
    for c in range(NCORES):
        m = dict(shared)
        m["x_tok"] = np.ascontiguousarray(x[c])
        m["xT"] = np.ascontiguousarray(x[c].T)
        m["upT_sl"] = np.ascontiguousarray(upT[c * 1024:(c + 1) * 1024, :])
        m["dnT_sl"] = np.ascontiguousarray(dnT[c * 256:(c + 1) * 256, :])
        in_maps.append(m)

    nc = build_nc()
    res = run_bass_kernel_spmd(nc, in_maps, core_ids=list(range(NCORES)))
    out = np.stack([np.ascontiguousarray(res.results[c]["outT"].T) for c in range(NCORES)], axis=0)
    return out


# version 28 (final)
# speedup vs baseline: 1.0990x; 1.0990x over previous
import contextlib
import numpy as np
import ml_dtypes
import concourse.bass as bass
import concourse.mybir as mybir
from concourse.bass_utils import run_bass_kernel_spmd

F32 = mybir.dt.float32
BF16 = mybir.dt.bfloat16
I32 = mybir.dt.int32
AF = mybir.ActivationFunctionType
ALU = mybir.AluOpType
AX = mybir.AxisListType

L, D, R, DH, FF = 4096, 2048, 2048, 512, 8192
LS = 512
NLS = L // LS
DT, RT, HTN, FT = D // 128, R // 128, DH // 128, FF // 128
NCORES = 8
EPS = 1e-5
GELU_C = 0.044715
GELU_S = 2.0 * 0.7978845608028654
SGN_MASK = -0x80000000
EXP_MASK = 0x7F800000
INV_BASE = 0x7F000000
MIDS = [0.25, 0.75, 1.25, 1.75, 2.5, 3.5, 5.0]

ENGS = ("sync", "tensor", "vector", "scalar", "gpsimd")


class Trk:
    __slots__ = ("h", "c", "name")

    def __init__(self, h, name):
        self.h = h
        self.c = 0
        self.name = name


class Sched:

    def __init__(self, vtrk=None):
        self.ops = {k: [] for k in ENGS}
        self.vtrk = vtrk

    def add(self, eng, fn, waits=(), inc=None, n=1):
        ws = [(t, v) for (t, v) in waits if v > 0]
        if eng == "vector" and self.vtrk is not None:
            if self.vtrk.c > 0:
                ws.append((self.vtrk, self.vtrk.c))
            if inc is None:
                inc, n = self.vtrk, 1
            elif inc is not self.vtrk:
                raise ValueError("vector ops must inc the vector track")
            else:
                n = 1
        if inc is not None:
            inc.c += n
        self.ops[eng].append((fn, ws, inc, n))


def aft(trk):
    return (trk, trk.c)


def build_nc():
    nc = bass.Bass("TRN2", target_bir_lowering=False)
    ap = lambda t: t.ap()

    d_in = {}
    for name, shape, dt in [
        ("x_tok", [L, D], F32),
        ("xT", [D, L], F32),
        ("W2u", [16, 128, 16, 128], F32),
        ("W2v", [16, 128, 16, 128], F32),
        ("Wg1T", [D, DH], F32),
        ("Wg2T", [DH, R], F32),
        ("upT_sl", [FF // 8, D], F32),
        ("dnT_sl", [D // 8, FF], F32),
        ("bu_p", [128, 16], F32),
        ("bv_p", [128, 16], F32),
        ("bg1_p", [128, 4], F32),
        ("bg2_p", [128, 16], F32),
        ("cb_p", [128, 16], F32),
        ("db_p", [128, 16], F32),
        ("ub_p", [128, 64], F32),
        ("cw_p", [128, 48], F32),
        ("ident", [128, 128], BF16),
    ]:
        d_in[name] = ap(nc.dram_tensor(name, shape, dt, kind="ExternalInput"))
    d_out = ap(nc.dram_tensor("outT", [D, L], F32, kind="ExternalOutput"))

    xnT_scr = ap(nc.dram_tensor("xnT_scr", [32, 16, 128, 128], BF16))
    s_scr = ap(nc.dram_tensor("s_scr", [128, 16, L], F32))
    w2u_scr = ap(nc.dram_tensor("w2u_scr", [16, 128, 2048], BF16))
    w2v_scr = ap(nc.dram_tensor("w2v_scr", [16, 128, 2048], BF16))
    q_up_in = ap(nc.dram_tensor("q_up_in", [1024, 2048], BF16))
    q_dn_in = ap(nc.dram_tensor("q_dn_in", [256, 8192], BF16))
    q_up = ap(nc.dram_tensor("q_up", [8192, 2048], BF16))
    q_dn = ap(nc.dram_tensor("q_dn", [2048, 8192], BF16))

    with contextlib.ExitStack() as stack:
        e = stack.enter_context

        tS = Trk(e(nc.semaphore("tS")), "tS")
        tWU = [Trk(e(nc.semaphore(f"tWU{i}")), f"tWU{i}") for i in range(3)]
        tWV = [Trk(e(nc.semaphore(f"tWV{i}")), f"tWV{i}") for i in range(3)]
        tWZ = [Trk(e(nc.semaphore(f"tWZ{i}")), f"tWZ{i}") for i in range(3)]
        tUPs = [Trk(e(nc.semaphore(f"tUPs{i}")), f"tUPs{i}") for i in range(3)]
        tDNs = [Trk(e(nc.semaphore(f"tDNs{i}")), f"tDNs{i}") for i in range(2)]
        tXN = Trk(e(nc.semaphore("tXN")), "tXN")
        tXF = Trk(e(nc.semaphore("tXF")), "tXF")
        tSFs = [Trk(e(nc.semaphore(f"tSF{i}")), f"tSF{i}") for i in range(2)]
        tQS = Trk(e(nc.semaphore("tQS")), "tQS")
        tACC = Trk(e(nc.semaphore("tACC")), "tACC")
        tCV = [Trk(e(nc.semaphore(f"tCV{i}")), f"tCV{i}") for i in range(2)]
        tCS = [Trk(e(nc.semaphore(f"tCS{i}")), f"tCS{i}") for i in range(2)]
        tG = Trk(e(nc.semaphore("tG")), "tG")
        tC = Trk(e(nc.semaphore("tC")), "tC")
        tT = Trk(e(nc.semaphore("tT")), "tT")
        tV = Trk(e(nc.semaphore("tV")), "tV")
        tSc = Trk(e(nc.semaphore("tSc")), "tSc")
        tG2 = Trk(e(nc.semaphore("tG2")), "tG2")

        sched = Sched(vtrk=tV)
        S = sched.add

        pH = e(nc.psum_tensor("pH", [128, 1024], F32))
        pU = [e(nc.psum_tensor(f"pU{i}", [128, 512], F32)) for i in range(2)]
        pV = [e(nc.psum_tensor(f"pV{i}", [128, 512], F32)) for i in range(2)]
        pG = [e(nc.psum_tensor(f"pG{i}", [128, 512], F32)) for i in range(2)]

        ident = e(nc.sbuf_tensor("identb", [128, 128], BF16))
        bu_p = e(nc.sbuf_tensor("bu_pb", [128, 16], F32))
        bv_p = e(nc.sbuf_tensor("bv_pb", [128, 16], F32))
        bg1_p = e(nc.sbuf_tensor("bg1_pb", [128, 4], F32))
        bg2_p = e(nc.sbuf_tensor("bg2_pb", [128, 16], F32))
        cb_p = e(nc.sbuf_tensor("cb_pb", [128, 16], F32))
        db_p = e(nc.sbuf_tensor("db_pb", [128, 16], F32))
        ub_p = e(nc.sbuf_tensor("ub_pb", [128, 64], F32))
        cw_p = e(nc.sbuf_tensor("cw_pb", [128, 48], F32))

        for name, buf in [("ident", ident), ("bu_p", bu_p), ("bv_p", bv_p),
                          ("bg1_p", bg1_p), ("bg2_p", bg2_p), ("cb_p", cb_p),
                          ("db_p", db_p), ("ub_p", ub_p), ("cw_p", cw_p)]:
            S("sync", lambda e_, a=d_in[name], b=buf: e_.dma_start(out=b[:], in_=a),
              inc=tS, n=16)
        small_ld = aft(tS)

        S("gpsimd", lambda e_: e_.dma_start(out=d_out, in_=d_in["xT"]), inc=tG, n=16)
        prefill_done = aft(tG)

        with contextlib.ExitStack() as p0:
            e0 = p0.enter_context
            cvt = [e0(nc.sbuf_tensor(f"cvt{i}", [128, 2048], BF16)) for i in range(2)]
            xln = [e0(nc.sbuf_tensor(f"xln{i}", [128, 2048], F32)) for i in range(2)]
            xsq = [e0(nc.sbuf_tensor(f"xsq{i}", [128, 2048], F32)) for i in range(2)]
            xnbf = [e0(nc.sbuf_tensor(f"xnbf{i}", [128, 2048], BF16)) for i in range(2)]
            stg = [e0(nc.sbuf_tensor(f"stg{i}", [128, 2048], BF16)) for i in range(2)]
            ls1 = e0(nc.sbuf_tensor("ls1", [128, 1], F32))
            ls2 = e0(nc.sbuf_tensor("ls2", [128, 1], F32))
            lm = e0(nc.sbuf_tensor("lm", [128, 1], F32))
            lv = e0(nc.sbuf_tensor("lv", [128, 1], F32))
            lsd = e0(nc.sbuf_tensor("lsd", [128, 1], F32))
            lrs = e0(nc.sbuf_tensor("lrs", [128, 1], F32))

            cvt_st = {}
            blocks = [(d_in["W2u"], w2u_scr, bt) for bt in range(16)] + \
                     [(d_in["W2v"], w2v_scr, bt) for bt in range(16)]
            for i, (srct, dst, bt) in enumerate(blocks):
                sl = i % 2
                w = [cvt_st[sl]] if sl in cvt_st else []
                S("gpsimd", lambda e_, a=srct, b=cvt[sl], t=bt: e_.dma_start(
                    out=b[:], in_=a[t].rearrange("p a b -> p (a b)")), waits=w, inc=tCV[sl], n=16)
                S("sync", lambda e_, a=cvt[sl], b=dst, t=bt: e_.dma_start(
                    out=b[t], in_=a[:]), waits=[aft(tCV[sl])],
                  inc=tCS[sl], n=16)
                cvt_st[sl] = aft(tCS[sl])
            wcvt_done_S = aft(tCS[0])
            wcvt_done_S2 = aft(tCS[1])

            xld = {}
            xn_done = {}
            tr_done = {}
            ev_done = {}
            st_done = {}
            sqd = {}
            sqrt_done = {}
            for lt in range(32):
                sl = lt % 2
                w = [st_done[lt - 2]] if lt - 2 in st_done else []
                if lt - 2 in xn_done:
                    w.append(xn_done[lt - 2])
                if lt - 2 in sqd:
                    w.append(sqd[lt - 2])
                S("sync", lambda e_, b=xln[sl], t=lt: e_.dma_start(
                    out=b[:], in_=d_in["x_tok"][t * 128:(t + 1) * 128, :]),
                  waits=w, inc=tS, n=16)
                xld[lt] = aft(tS)
                S("vector", lambda e_, b=xln[sl]: e_.reduce_sum(ls1[:], b[:], AX.X),
                  waits=[xld[lt]])
                S("scalar", lambda e_, b=xln[sl], o=xsq[sl]: e_.activation(
                    o[:], b[:], AF.Square),
                  waits=[xld[lt]] + ([xn_done[lt - 2]] if lt - 2 in xn_done else []),
                  inc=tSc)
                sqd[lt] = aft(tSc)
                S("vector", lambda e_, b=xsq[sl]: e_.reduce_sum(ls2[:], b[:], AX.X),
                  waits=[sqd[lt]])
                S("vector", lambda e_: e_.tensor_single_scalar(lm[:], ls1[:], 1.0 / D, ALU.mult))
                S("vector", lambda e_: e_.tensor_scalar(lv[:], ls2[:], 1.0 / D, EPS, ALU.mult, ALU.add),
                  waits=([sqrt_done[lt - 1]] if lt - 1 in sqrt_done else []))
                S("vector", lambda e_: e_.tensor_tensor(ls1[:], lm[:], lm[:], ALU.mult))
                S("vector", lambda e_: e_.tensor_tensor(lv[:], lv[:], ls1[:], ALU.subtract),
                  inc=tV)
                var_done = aft(tV)
                S("scalar", lambda e_: e_.activation(lsd[:], lv[:], AF.Sqrt),
                  waits=[var_done], inc=tSc)
                sqrt_done[lt] = aft(tSc)
                S("vector", lambda e_: e_.reciprocal(lrs[:], lsd[:]), waits=[sqrt_done[lt]])
                S("vector", lambda e_, b=xln[sl], o=xnbf[sl]: e_.tensor_scalar(
                    o[:], b[:], lm[:, 0:1], lrs[:, 0:1], ALU.subtract, ALU.mult),
                  waits=([tr_done[(lt - 2, 3)]] if (lt - 2, 3) in tr_done else []), inc=tV)
                xn_done[lt] = aft(tV)
                pHb = pH[:].bitcast(BF16)
                for gq in range(4):
                    wq = [xn_done[lt]]
                    key = (lt, gq)
                    pkey = (lt, gq - 2) if gq >= 2 else (lt - 1, gq + 2)
                    if pkey in ev_done:
                        wq.append(ev_done[pkey])
                    for j in range(4):
                        dtt = gq * 4 + j
                        S("tensor", lambda e_, src=xnbf[sl], o=pHb, g=gq, jj=j, d2=dtt: e_.transpose(
                            o[:, (g % 2) * 1024 + jj * 128:(g % 2) * 1024 + jj * 128 + 128],
                            src[:, d2 * 128:(d2 + 1) * 128], ident[:]),
                          waits=(wq if j == 0 else []), inc=(tT if j == 3 else None))
                    tr_done[key] = aft(tT)
                    S("scalar", lambda e_, o=stg[sl], g=gq, ph=pHb: e_.copy(
                        o[:, g * 512:(g + 1) * 512], ph[:, (g % 2) * 1024:(g % 2) * 1024 + 512]),
                      waits=[tr_done[key]], inc=tSc)
                    ev_done[key] = aft(tSc)
                S("sync", lambda e_, b=stg[sl], t=lt: e_.dma_start(
                    out=xnT_scr[t].rearrange("a p c -> p a c"),
                    in_=b[:].rearrange("p (a c) -> p a c", c=128)),
                  waits=[ev_done[(lt, 3)]], inc=tS, n=16)
                st_done[lt] = aft(tS)
            xnT_all_stored = aft(tS)

            with contextlib.ExitStack() as q0:
                eq = q0.enter_context
                qbuf = {}
                for en in ("vector", "gpsimd"):
                    qbuf[en] = dict(
                        qin=[eq(nc.sbuf_tensor(f"qin_{en}{i}", [128, 512], F32)) for i in range(2)],
                        z=eq(nc.sbuf_tensor(f"z_{en}", [128, 512], F32)),
                        za=eq(nc.sbuf_tensor(f"za_{en}", [128, 512], F32)),
                        cnt=[eq(nc.sbuf_tensor(f"cnt_{en}{i}", [128, 512], F32)) for i in range(2)],
                        b1=eq(nc.sbuf_tensor(f"b1_{en}", [128, 512], F32)),
                        b2=eq(nc.sbuf_tensor(f"b2_{en}", [128, 512], F32)),
                        b3=eq(nc.sbuf_tensor(f"b3_{en}", [128, 512], F32)),
                        sg=eq(nc.sbuf_tensor(f"sg_{en}", [128, 512], I32)),
                        qo=[eq(nc.sbuf_tensor(f"qo_{en}{i}", [128, 512], BF16)) for i in range(2)],
                        am=eq(nc.sbuf_tensor(f"am_{en}", [128, 16], F32)),
                        eb=eq(nc.sbuf_tensor(f"eb_{en}", [128, 16], I32)),
                        isc=eq(nc.sbuf_tensor(f"isc_{en}", [128, 16], F32)),
                        scl=eq(nc.sbuf_tensor(f"scl_{en}", [128, 16], F32)),
                    )
                qstg = [eq(nc.sbuf_tensor(f"qstg{i}", [128, 512], BF16)) for i in range(2)]

                tiles = []
                for pt in range(8):
                    for dc in range(4):
                        tiles.append(("up", pt, dc))
                for pt in range(2):
                    for fc in range(16):
                        tiles.append(("dn", pt, fc))

                qld = {}
                qdone = {}
                trq_done = {}
                evq_done = {}
                stq_done = {}
                eng_slot = {"vector": 0, "gpsimd": 0}
                for ti, (kind, pt, cc) in enumerate(tiles):
                    en = "vector"
                    tE = tV if en == "vector" else tG2
                    bb = qbuf[en]
                    sl = eng_slot[en] % 2
                    eng_slot[en] += 1
                    if kind == "up":
                        src = d_in["upT_sl"][pt * 128:(pt + 1) * 128, cc * 512:(cc + 1) * 512]
                    else:
                        src = d_in["dnT_sl"][pt * 128:(pt + 1) * 128, cc * 512:(cc + 1) * 512]
                    w = []
                    prev_same = [k for k in range(ti) if (("vector" if k % 3 != 2 else "gpsimd") == en)]
                    if len(prev_same) >= 2:
                        w = [qdone[prev_same[-2]]]
                    S("sync", lambda e_, a=src, b=bb["qin"][sl]: e_.dma_start(out=b[:], in_=a),
                      waits=w, inc=tS, n=16)
                    qld[ti] = aft(tS)
                    qin, z, za, cnt = bb["qin"][sl], bb["z"], bb["za"], bb["cnt"]
                    b1, b2, b3, sg, qo = bb["b1"], bb["b2"], bb["b3"], bb["sg"], bb["qo"][sl]
                    am, eb, isc, scl = bb["am"], bb["eb"], bb["isc"], bb["scl"]
                    q3 = lambda t: t[:].rearrange("p (b k) -> p b k", k=32)
                    wr = [qld[ti]]
                    if en != "vector" and prev_same:
                        wr.append(qdone[prev_same[-1]])
                    S("vector", lambda e_, q=qin, o=am: e_.tensor_reduce(
                        o[:], q3(q), AX.X, ALU.max, apply_absolute_value=True),
                      waits=wr)
                    S("vector", lambda e_, o=am: e_.tensor_single_scalar(o[:], o[:], 1e-30, ALU.max))
                    S("vector", lambda e_, a=am, o=eb: e_.tensor_single_scalar(
                        o[:], a[:].bitcast(I32), EXP_MASK, ALU.bitwise_and))
                    S("vector", lambda e_, a=eb, o=scl: e_.tensor_single_scalar(
                        o[:], a[:].bitcast(F32), 0.25, ALU.mult))
                    S("vector", lambda e_, a=scl, o=isc: e_.reciprocal(o[:], a[:]))
                    S("vector", lambda e_, q=qin, o=sg: e_.tensor_single_scalar(
                        o[:], q[:].bitcast(I32), SGN_MASK, ALU.bitwise_and), inc=tV)
                    scales_done = aft(tV)
                    S(en, lambda e_, q=qin, a=isc, o=z: e_.tensor_tensor(
                        q3(o), q3(q), a[:].broadcast_to([128, 16, 32]), ALU.mult),
                      waits=([scales_done] if en != "vector" else []), inc=tE)
                    z_done = aft(tE)
                    S("scalar", lambda e_, a=z, o=za: e_.activation(o[:], a[:], AF.Abs),
                      waits=[z_done], inc=tSc)
                    za_done = aft(tSc)
                    S(en, lambda e_, a=za, o=cnt[0]: e_.tensor_single_scalar(
                        o[:], a[:], MIDS[0], ALU.is_ge), waits=[za_done])
                    for mi, mid in enumerate(MIDS[1:]):
                        src_c, dst_c = cnt[mi % 2], cnt[(mi + 1) % 2]
                        S(en, lambda e_, a=src_c, b=dst_c, m=mid, x=za: e_.scalar_tensor_tensor(
                            b[:], x[:], m, a[:], ALU.is_ge, ALU.add))
                    cfin = cnt[(len(MIDS) - 1) % 2]
                    S(en, lambda e_, c=cfin, o=b1: e_.tensor_single_scalar(o[:], c[:], 0.5, ALU.mult))
                    S(en, lambda e_, c=cfin, o=b2: e_.tensor_single_scalar(o[:], c[:], -2.0, ALU.add))
                    S(en, lambda e_, a=b1, b=b2, o=b3: e_.tensor_tensor(o[:], a[:], b[:], ALU.max))
                    S(en, lambda e_, c=cfin, o=b1: e_.tensor_scalar(
                        o[:], c[:], 2.0, -8.0, ALU.mult, ALU.add))
                    S(en, lambda e_, a=b3, b=b1, o=b2: e_.tensor_tensor(o[:], a[:], b[:], ALU.max))
                    S(en, lambda e_, a=b2, o=b1: e_.tensor_single_scalar(o[:], a[:], 6.0, ALU.min))
                    S(en, lambda e_, a=b1, s2=scl, o=b2: e_.tensor_tensor(
                        o[:].rearrange("p (b k) -> p b k", k=32),
                        a[:].rearrange("p (b k) -> p b k", k=32),
                        s2[:].broadcast_to([128, 16, 32]), ALU.mult), inc=tE)
                    deq_done = aft(tE)
                    S("vector", lambda e_, a=b2, b=sg, o=b3: e_.tensor_tensor(
                        o[:].bitcast(I32), a[:].bitcast(I32), b[:], ALU.bitwise_or),
                      waits=([deq_done] if en != "vector" else []))
                    S("vector", lambda e_, a=b3, o=qo: e_.tensor_copy(o[:], a[:]), inc=tV)
                    qdone[ti] = aft(tV)
                    pq = pU[ti % 2][:].bitcast(BF16)
                    wq = [qdone[ti]]
                    if ti - 2 in evq_done:
                        wq.append(evq_done[ti - 2])
                    for j in range(4):
                        S("tensor", lambda e_, o=pq, src=qo, jj=j: e_.transpose(
                            o[:, jj * 128:(jj + 1) * 128], src[:, jj * 128:(jj + 1) * 128], ident[:]),
                          waits=(wq if j == 0 else []), inc=(tT if j == 3 else None))
                    trq_done[ti] = aft(tT)
                    wq = [trq_done[ti]]
                    if ti - 2 in stq_done:
                        wq.append(stq_done[ti - 2])
                    S("scalar", lambda e_, o=qstg[ti % 2], p=pq: e_.copy(o[:], p[:, 0:512]),
                      waits=wq, inc=tSc)
                    evq_done[ti] = aft(tSc)
                    if kind == "up":
                        dst = q_up_in[pt * 128:(pt + 1) * 128, cc * 512:(cc + 1) * 512]
                    else:
                        dst = q_dn_in[pt * 128:(pt + 1) * 128, cc * 512:(cc + 1) * 512]
                    S("sync", lambda e_, b=qstg[ti % 2], d2=dst: e_.dma_start(
                        out=d2.rearrange("p (j f) -> p j f", f=128),
                        in_=b[:].rearrange("p (j f) -> p j f", f=128)),
                      waits=[evq_done[ti]], inc=tQS, n=16)
                    stq_done[ti] = aft(tQS)
                q_in_stored = aft(tQS)


        p0S, p0T, p0V, p0Sc, p0G, p0G2 = aft(tS), aft(tT), aft(tV), aft(tSc), aft(tG), aft(tG2)
        fence0 = [p0S, p0T, p0V, p0Sc, p0G, p0G2, aft(tQS), aft(tCS[0]), aft(tCS[1])]

        with contextlib.ExitStack() as pr:
            er = pr.enter_context
            Wg1T_sb = er(nc.sbuf_tensor("Wg1T_sb", [128, 16, 512], BF16))
            Wg2T_sb = er(nc.sbuf_tensor("Wg2T_sb", [128, 4, 2048], BF16))
            diag_sb = er(nc.sbuf_tensor("diag_sb", [128, 48, 128], BF16))
            xnT_sb = er(nc.sbuf_tensor("xnT_sb", [128, 16, 768], BF16))
            hT_sb = er(nc.sbuf_tensor("hT_sb", [128, 4, 512], BF16))
            sigT = [er(nc.sbuf_tensor(f"sigT{i}", [128, 512], F32)) for i in range(2)]
            ugT = [er(nc.sbuf_tensor(f"ugT{i}", [128, 512], F32)) for i in range(2)]
            kcT = [er(nc.sbuf_tensor(f"kcT{i}", [128, 512], F32)) for i in range(2)]
            vgT = [er(nc.sbuf_tensor(f"vgT{i}", [128, 512], BF16)) for i in range(2)]
            gvT = er(nc.sbuf_tensor("gvT", [128, 16, 512], BF16))
            s_stage = er(nc.sbuf_tensor("s_stage", [128, 16, 512], F32))
            wu_ring = er(nc.sbuf_tensor("wu_ring", [128, 3, 16, 128], BF16))
            wv_ring = er(nc.sbuf_tensor("wv_ring", [128, 3, 16, 128], BF16))
            wz_ring = er(nc.sbuf_tensor("wz_ring", [128, 3, 16, 128], BF16))
            carry = er(nc.sbuf_tensor("carry", [128, 16], F32))

            S("gpsimd", lambda e_: e_.dma_start(
                out=Wg1T_sb[:], in_=d_in["Wg1T"].rearrange("(b p) h -> p b h", p=128)),
              waits=fence0, inc=tG, n=16)
            S("gpsimd", lambda e_: e_.dma_start(
                out=Wg2T_sb[:], in_=d_in["Wg2T"].rearrange("(b p) r -> p b r", p=128)),
              inc=tG, n=16)
            wg_ld = aft(tG)
            S("gpsimd", lambda e_: e_.collective_compute(
                "AllGather", ALU.bypass, replica_groups=[list(range(NCORES))],
                ins=[q_up_in.opt()], outs=[q_up.opt()]),
              waits=[q_in_stored], inc=tC)
            S("gpsimd", lambda e_: e_.collective_compute(
                "AllGather", ALU.bypass, replica_groups=[list(range(NCORES))],
                ins=[q_dn_in.opt()], outs=[q_dn.opt()]),
              inc=tC)
            cc_done = aft(tC)
            S("vector", lambda e_: e_.memset(carry[:], 0.0), waits=fence0, inc=tV)
            for k in range(3):
                for dtt in range(16):
                    S("vector", lambda e_, kk=k, d2=dtt: e_.tensor_scalar(
                        diag_sb[:, kk * 16 + d2, :], ident[:], cw_p[:, kk * 16 + d2:kk * 16 + d2 + 1],
                        None, ALU.mult), waits=[small_ld], inc=(tV if (k, dtt) == (2, 15) else None))
            diag_done = aft(tV)

            scr_r = xnT_scr.rearrange("a b p c -> p b a c")

            wu_ld, wv_ld, wz_ld = {}, {}, {}
            u_done, v_done, g_done = {}, {}, {}
            sig_done, ug_done, vg_done, gv_done = {}, {}, {}, {}
            uv_done, scp_done = {}, {}
            relu_done = {}
            s_store = {}
            xnT_release = (tT, 0)
            rt_end = {}
            uv_end_prev = (tT, 0)
            g_mms_prev = (tT, 0)
            memset_done = None

            def wu_dma(ls, rt, extra=()):
                slot = rt % 3
                S("sync", lambda e_, r2=rt, s2=slot: e_.dma_start(
                    out=wu_ring[:, s2, :, :].rearrange("p a b -> p (a b)"), in_=w2u_scr[r2]),
                  waits=list(extra), inc=tWU[slot], n=16)
                wu_ld[(ls, rt)] = aft(tWU[slot])
                S("sync", lambda e_, r2=rt, s2=slot: e_.dma_start(
                    out=wv_ring[:, s2, :, :].rearrange("p a b -> p (a b)"), in_=w2v_scr[r2]),
                  inc=tWV[slot], n=16)
                wv_ld[(ls, rt)] = aft(tWV[slot])

            def wz_dma(ls, dtt, extra=()):
                slot = dtt % 3
                S("sync", lambda e_, d2=dtt, s2=slot: e_.dma_start(
                    out=wz_ring[:, s2, :, :].rearrange("p a b -> p (a b)"), in_=w2u_scr[d2]),
                  waits=list(extra), inc=tWZ[slot], n=16)
                wz_ld[(ls, dtt)] = aft(tWZ[slot])

            for ls in range(NLS):
                lt0 = ls * 4
                w = [xnT_release]
                if ls == 0:
                    w.extend(fence0)
                for i in range(4):
                    S("sync", lambda e_, t=lt0 + i, o=128 + i * 128: e_.dma_start(
                        out=xnT_sb[:, :, o:o + 128], in_=scr_r[:, :, t, :]),
                      waits=(w if i == 0 else []), inc=tXN, n=16)
                if ls > 0:
                    S("sync", lambda e_, t=lt0 - 1: e_.dma_start(
                        out=xnT_sb[:, :, 0:128], in_=scr_r[:, :, t, :]), inc=tXN, n=16)
                else:
                    S("vector", lambda e_: e_.memset(xnT_sb[:, :, 127:128], 0.0),
                      waits=[xnT_release], inc=tV)
                    memset_done = aft(tV)
                if ls < NLS - 1:
                    S("sync", lambda e_, t=lt0 + 4: e_.dma_start(
                        out=xnT_sb[:, :, 640:768], in_=scr_r[:, :, t, :]), inc=tXN, n=16)
                else:
                    S("vector", lambda e_: e_.memset(xnT_sb[:, :, 640:641], 0.0),
                      waits=[xnT_release], inc=tV)
                    memset_done = aft(tV)
                xnT_ld = aft(tXN)

                prev_user = {0: 15, 1: 13, 2: 14}
                if ls == 0:
                    for rt in range(3):
                        wu_dma(ls, rt, extra=[wcvt_done_S, wcvt_done_S2] + fence0)
                    for dtt in range(3):
                        wz_dma(ls, dtt, extra=([wcvt_done_S, wcvt_done_S2] + fence0 if dtt == 0 else []))
                else:
                    for rt in range(3):
                        wu_dma(ls, rt, extra=[rt_end[(ls - 1, prev_user[rt])]])
                    for dtt in range(3):
                        wz_dma(ls, dtt, extra=[uv_done[(ls - 1, prev_user[dtt])]])

                for ht in range(4):
                    half = (ht % 2)
                    w = [xnT_ld]
                    if ls == 0 and ht == 0:
                        w += [wg_ld, p0V, p0Sc, p0G2, diag_done]
                    if (ls, ht - 2) in relu_done:
                        w.append(relu_done[(ls, ht - 2)])
                    if ls > 0 and ht < 2:
                        w.append(relu_done[(ls - 1, ht + 2)])
                    for dtt in range(16):
                        S("tensor", lambda e_, d2=dtt, h2=ht, hf=half: e_.matmul(
                            pH[:, hf * 512:hf * 512 + 512],
                            Wg1T_sb[:, d2, h2 * 128:(h2 + 1) * 128],
                            xnT_sb[:, d2, 128:640],
                            start=(d2 == 0), stop=(d2 == 15)),
                          waits=(w if dtt == 0 else []), inc=(tT if dtt == 15 else None))
                    h_mm = aft(tT)
                    w2 = [h_mm, small_ld]
                    w2.append(g_mms_prev)
                    S("scalar", lambda e_, h2=ht, hf=half: e_.activation(
                        hT_sb[:, h2, :], pH[:, hf * 512:hf * 512 + 512], AF.Relu,
                        bias=bg1_p[:, h2:h2 + 1]), waits=w2, inc=tSc)
                    relu_done[(ls, ht)] = aft(tSc)

                for rt in range(16):
                    par = rt % 2
                    w = [wu_ld[(ls, rt)], xnT_ld]
                    if (ls, rt - 2) in ug_done:
                        w.append(ug_done[(ls, rt - 2)])
                    elif ls > 0 and rt < 2:
                        w.append(ug_done[(ls - 1, rt + 14)])
                    for dtt in range(16):
                        S("tensor", lambda e_, d2=dtt, r2=rt: e_.matmul(
                            pU[r2 % 2][:], wu_ring[:, r2 % 3, d2, :], xnT_sb[:, d2, 128:640],
                            start=(d2 == 0), stop=(d2 == 15)),
                          waits=(w if dtt == 0 else []), inc=(tT if dtt == 15 else None))
                    u_done[(ls, rt)] = aft(tT)
                    w = [wv_ld[(ls, rt)]]
                    if (ls, rt - 2) in vg_done:
                        w.append(vg_done[(ls, rt - 2)])
                    elif ls > 0 and rt < 2:
                        w.append(vg_done[(ls - 1, rt + 14)])
                    for dtt in range(16):
                        S("tensor", lambda e_, d2=dtt, r2=rt: e_.matmul(
                            pV[r2 % 2][:], wv_ring[:, r2 % 3, d2, :], xnT_sb[:, d2, 128:640],
                            start=(d2 == 0), stop=(d2 == 15)),
                          waits=(w if dtt == 0 else []), inc=(tT if dtt == 15 else None))
                    v_done[(ls, rt)] = aft(tT)
                    w = [relu_done[(ls, 3)]]
                    if (ls, rt - 2) in sig_done:
                        w.append(sig_done[(ls, rt - 2)])
                    elif ls > 0 and rt < 2:
                        w.append(sig_done[(ls - 1, rt + 14)])
                    for ht in range(4):
                        S("tensor", lambda e_, h2=ht, r2=rt: e_.matmul(
                            pG[r2 % 2][:], Wg2T_sb[:, h2, r2 * 128:(r2 + 1) * 128], hT_sb[:, h2, :],
                            start=(h2 == 0), stop=(h2 == 3)),
                          waits=(w if ht == 0 else []), inc=(tT if ht == 3 else None))
                    g_done[(ls, rt)] = aft(tT)
                    if rt + 3 < 16:
                        wu_dma(ls, rt + 3, extra=[u_done[(ls, rt)], v_done[(ls, rt)]])
                    w = [g_done[(ls, rt)]]
                    if (ls, rt - 2) in vg_done:
                        w.append(vg_done[(ls, rt - 2)])
                    elif ls > 0 and rt < 2:
                        w.append(vg_done[(ls - 1, rt + 14)])
                    S("scalar", lambda e_, r2=rt: e_.activation(
                        sigT[r2 % 2][:], pG[r2 % 2][:], AF.Sigmoid, bias=bg2_p[:, r2:r2 + 1]),
                      waits=w, inc=tSc)
                    sig_done[(ls, rt)] = aft(tSc)
                    S("vector", lambda e_, r2=rt: e_.scalar_tensor_tensor(
                        ugT[r2 % 2][:], pU[r2 % 2][:], bu_p[:, r2:r2 + 1], sigT[r2 % 2][:],
                        ALU.add, ALU.mult),
                      waits=[u_done[(ls, rt)], sig_done[(ls, rt)]], inc=tV)
                    ug_done[(ls, rt)] = aft(tV)
                    for rep in range(2):
                        S("vector", lambda e_, r2=rt: e_.tensor_tensor_scan(
                            kcT[r2 % 2][:], ugT[r2 % 2][:], ugT[r2 % 2][:], carry[:, r2:r2 + 1],
                            ALU.add, ALU.bypass))
                    S("vector", lambda e_, r2=rt: e_.tensor_copy(
                        carry[:, r2:r2 + 1], kcT[r2 % 2][:, 511:512]))
                    S("vector", lambda e_, r2=rt: e_.scalar_tensor_tensor(
                        vgT[r2 % 2][:], pV[r2 % 2][:], bv_p[:, r2:r2 + 1], sigT[r2 % 2][:],
                        ALU.add, ALU.mult),
                      waits=[v_done[(ls, rt)]], inc=tV)
                    vg_done[(ls, rt)] = aft(tV)
                    wgv = []
                    if rt == 0:
                        wgv.append(uv_end_prev)
                    S("vector", lambda e_, r2=rt: e_.tensor_tensor(
                        gvT[:, r2, :], kcT[r2 % 2][:], vgT[r2 % 2][:], ALU.mult),
                      waits=wgv, inc=tV)
                    gv_done[(ls, rt)] = aft(tV)
                    rt_end[(ls, rt)] = aft(tT)
                g_mms_prev = aft(tT)

                for dtt in range(16):
                    par = dtt % 2
                    w0 = [wz_ld[(ls, dtt)], vg_done[(ls, 14 + par)]]
                    if (ls, dtt - 2) in scp_done:
                        w0.append(scp_done[(ls, dtt - 2)])
                    if memset_done is not None:
                        w0.append(memset_done)
                    for rt in range(16):
                        w = list(w0) if rt == 0 else [gv_done[(ls, rt)]]
                        if rt == 0:
                            w.append(gv_done[(ls, 0)])
                        S("tensor", lambda e_, d2=dtt, r2=rt: e_.matmul(
                            pV[d2 % 2][:], wz_ring[:, d2 % 3, r2, :], gvT[:, r2, :],
                            start=(r2 == 0), stop=False),
                          waits=w, inc=None)
                    for k in range(3):
                        off = 127 + k
                        S("tensor", lambda e_, d2=dtt, kk=k, o=off: e_.matmul(
                            pV[d2 % 2][:], diag_sb[:, kk * 16 + d2, :], xnT_sb[:, d2, o:o + 512],
                            start=False, stop=(kk == 2)),
                          inc=(tT if k == 2 else None))
                    uv_done[(ls, dtt)] = aft(tT)
                    if dtt + 3 < 16:
                        wz_dma(ls, dtt + 3, extra=[uv_done[(ls, dtt)]])
                    w = [uv_done[(ls, dtt)]]
                    if ls - 1 in s_store:
                        w.append(s_store[ls - 1])
                    S("scalar", lambda e_, d2=dtt: e_.activation(
                        s_stage[:, d2, :], pV[d2 % 2][:], AF.Identity, bias=cb_p[:, d2:d2 + 1]),
                      waits=w, inc=tSc)
                    scp_done[(ls, dtt)] = aft(tSc)
                uv_end_prev = aft(tT)
                xnT_release = aft(tT)

                S("sync", lambda e_, l2=ls: e_.dma_start(
                    out=s_scr[:, :, l2 * 512:(l2 + 1) * 512], in_=s_stage[:]),
                  waits=[scp_done[(ls, 15)]], inc=tS, n=16)
                s_store[ls] = aft(tS)

        prS, prT, prV, prSc, prG = aft(tS), aft(tT), aft(tV), aft(tSc), aft(tG)
        fenceR = [prS, prT, prV, prSc, prG, aft(tXN)] + \
            [aft(t) for t in tWU + tWV + tWZ]

        with contextlib.ExitStack() as pf:
            ef = pf.enter_context
            xnF = ef(nc.sbuf_tensor("xnF", [128, 16, 512], BF16))
            sF = [ef(nc.sbuf_tensor(f"sF{i}", [128, 16, 512], F32)) for i in range(2)]
            aT = ef(nc.sbuf_tensor("aT", [128, 64, 512], BF16))
            up_ring = ef(nc.sbuf_tensor("up_ring", [128, 3, 16, 128], BF16))
            dn_ring = ef(nc.sbuf_tensor("dn_ring", [128, 2, 64, 128], BF16))
            xgf = [ef(nc.sbuf_tensor(f"xgf{i}", [128, 512], F32)) for i in range(2)]
            gA = [ef(nc.sbuf_tensor(f"gA{i}", [128, 512], F32)) for i in range(2)]
            gB = [ef(nc.sbuf_tensor(f"gB{i}", [128, 512], F32)) for i in range(2)]
            gt = [ef(nc.sbuf_tensor(f"gt{i}", [128, 512], F32)) for i in range(2)]
            gt2 = ef(nc.sbuf_tensor("gt2", [128, 512], F32))

            up_ld, dn_ld = {}, {}
            upm_done, a_done = {}, {}
            dnm_done, silu_done, sg_done = {}, {}, {}
            acc_done = {}
            xn_ld, s_ld = {}, {}
            up_phase_end = (tT, 0)
            dn_phase_end = (tT, 0)

            def up_dma(ls, ft, extra=()):
                S("sync", lambda e_, f2=ft: e_.dma_start(
                    out=up_ring[:, f2 % 3, :, :].rearrange("p a b -> p (a b)"),
                    in_=q_up[f2 * 128:(f2 + 1) * 128, :]),
                  waits=list(extra), inc=tUPs[ft % 3], n=16)
                up_ld[(ls, ft)] = aft(tUPs[ft % 3])

            def dn_dma(ls, dtt, extra=()):
                S("sync", lambda e_, d2=dtt: e_.dma_start(
                    out=dn_ring[:, d2 % 2, :, :].rearrange("p a b -> p (a b)"),
                    in_=q_dn[d2 * 128:(d2 + 1) * 128, :]),
                  waits=list(extra), inc=tDNs[dtt % 2], n=16)
                dn_ld[(ls, dtt)] = aft(tDNs[dtt % 2])

            for ls in range(NLS):
                lt0 = ls * 4
                w = [up_phase_end]
                if ls == 0:
                    w += fenceR + [cc_done]
                for i in range(4):
                    S("sync", lambda e_, t=lt0 + i, o=i * 128: e_.dma_start(
                        out=xnF[:, :, o:o + 128], in_=scr_r[:, :, t, :]),
                      waits=(w if i == 0 else []), inc=tXF, n=16)
                xn_ld[ls] = aft(tXF)
                w = []
                if ls - 2 in acc_done:
                    w.append(acc_done[ls - 2])
                if ls == 0:
                    w += fenceR
                S("sync", lambda e_, l2=ls, b=sF[ls % 2]: e_.dma_start(
                    out=b[:], in_=s_scr[:, :, l2 * 512:(l2 + 1) * 512]),
                  waits=w, inc=tSFs[ls % 2], n=16)
                s_ld[ls] = aft(tSFs[ls % 2])

                prev_user_up = {0: 63, 1: 61, 2: 62}
                if ls == 0:
                    for ft in range(3):
                        up_dma(ls, ft, extra=([cc_done] + fenceR if ft == 0 else []))
                else:
                    for ft in range(3):
                        up_dma(ls, ft, extra=[upm_done[(ls - 1, prev_user_up[ft])]])

                for ft in range(64):
                    par = ft % 2
                    w = [up_ld[(ls, ft)], xn_ld[ls]]
                    if (ls, ft - 2) in a_done:
                        w.append(a_done[(ls, ft - 2)])
                    elif ls > 0 and ft < 2:
                        w.append(a_done[(ls - 1, ft + 62)])
                    for dtt in range(16):
                        S("tensor", lambda e_, d2=dtt, f2=ft: e_.matmul(
                            pU[f2 % 2][:], up_ring[:, f2 % 3, d2, :], xnF[:, d2, :],
                            start=(d2 == 0), stop=(d2 == 15)),
                          waits=(w if dtt == 0 else []), inc=(tT if dtt == 15 else None))
                    upm_done[(ls, ft)] = aft(tT)
                    if ft + 3 < 64:
                        up_dma(ls, ft + 3, extra=[upm_done[(ls, ft)]])
                    w = [upm_done[(ls, ft)]]
                    if (ls, ft - 2) in a_done:
                        w.append(a_done[(ls, ft - 2)])
                    elif ls > 0 and ft < 2:
                        w.append(a_done[(ls - 1, ft + 62)])
                    S("scalar", lambda e_, f2=ft: e_.activation(
                        xgf[f2 % 2][:], pU[f2 % 2][:], AF.Identity, bias=ub_p[:, f2:f2 + 1]),
                      waits=w, inc=tSc)
                    xg_done = aft(tSc)
                    S("vector", lambda e_, f2=ft: e_.tensor_tensor(
                        gA[f2 % 2][:], xgf[f2 % 2][:], xgf[f2 % 2][:], ALU.mult),
                      waits=[xg_done])
                    S("vector", lambda e_, f2=ft: e_.tensor_scalar(
                        gB[f2 % 2][:], gA[f2 % 2][:], GELU_C * GELU_S, GELU_S, ALU.mult, ALU.add))
                    S("vector", lambda e_, f2=ft: e_.tensor_tensor(
                        gA[f2 % 2][:], gB[f2 % 2][:], xgf[f2 % 2][:], ALU.mult), inc=tV)
                    u_in_done = aft(tV)
                    S("scalar", lambda e_, f2=ft: e_.activation(
                        gB[f2 % 2][:], gA[f2 % 2][:], AF.Sigmoid),
                      waits=[u_in_done], inc=tSc)
                    sg_act = aft(tSc)
                    wv2 = [sg_act]
                    if ft == 0:
                        wv2.append(dn_phase_end)
                    S("vector", lambda e_, f2=ft: e_.tensor_tensor(
                        aT[:, f2, :], xgf[f2 % 2][:], gB[f2 % 2][:], ALU.mult),
                      waits=wv2, inc=tV)
                    a_done[(ls, ft)] = aft(tV)
                up_phase_end = aft(tT)

                if ls == 0:
                    dn_dma(ls, 0, extra=[cc_done] + fenceR)
                    dn_dma(ls, 1)
                else:
                    dn_dma(ls, 0, extra=[dnm_done[(ls - 1, 14)]])
                    dn_dma(ls, 1, extra=[dnm_done[(ls - 1, 15)]])
                for dtt in range(16):
                    par = dtt % 2
                    w0 = [dn_ld[(ls, dtt)]]
                    if (ls, dtt - 2) in sg_done:
                        w0.append(sg_done[(ls, dtt - 2)])
                    elif ls > 0 and dtt < 2:
                        w0.append(sg_done[(ls - 1, dtt + 14)])
                    for ft in range(64):
                        w = (w0 + [a_done[(ls, ft)]]) if ft == 0 else [a_done[(ls, ft)]]
                        S("tensor", lambda e_, d2=dtt, f2=ft: e_.matmul(
                            pV[d2 % 2][:], dn_ring[:, d2 % 2, f2, :], aT[:, f2, :],
                            start=(f2 == 0), stop=(f2 == 63)),
                          waits=w, inc=(tT if ft == 63 else None))
                    dnm_done[(ls, dtt)] = aft(tT)
                    if dtt + 2 < 16:
                        dn_dma(ls, dtt + 2, extra=[dnm_done[(ls, dtt)]])
                    w = [dnm_done[(ls, dtt)]]
                    if (ls, dtt - 2) in sg_done:
                        w.append(sg_done[(ls, dtt - 2)])
                    S("scalar", lambda e_, d2=dtt: e_.activation(
                        gt[d2 % 2][:], pV[d2 % 2][:], AF.Sigmoid, bias=db_p[:, d2:d2 + 1]),
                      waits=w, inc=tSc)
                    sig_dn = aft(tSc)
                    S("vector", lambda e_, d2=dtt: e_.scalar_tensor_tensor(
                        gt2[:], pV[d2 % 2][:], db_p[:, d2:d2 + 1], gt[d2 % 2][:],
                        ALU.add, ALU.mult),
                      waits=[sig_dn, s_ld[ls]])
                    S("vector", lambda e_, d2=dtt, b=sF[ls % 2]: e_.tensor_tensor(
                        b[:, d2, :], b[:, d2, :], gt2[:], ALU.mult), inc=tV)
                    sg_done[(ls, dtt)] = aft(tV)
                dn_phase_end = aft(tT)

                w = [sg_done[(ls, 15)], prefill_done]
                S("gpsimd", lambda e_, l2=ls, b=sF[ls % 2]: e_.dma_start(
                    out=d_out.rearrange("(b p) l -> p b l", p=128)[:, :, l2 * 512:(l2 + 1) * 512],
                    in_=b[:], accum_op=ALU.add),
                  waits=w, inc=tACC, n=16)
                acc_done[ls] = aft(tACC)

        S("gpsimd", lambda e_: e_.engine_nop(), waits=[aft(tG), aft(tACC)])
        S("sync", lambda e_: e_.nop(), waits=[aft(tS), aft(tG), aft(tC), aft(tXF)] +
          [aft(t) for t in tUPs + tDNs + tCV + tCS + tSFs])

        block = e(nc.Block())

        def runner(name):
            def run(engine):
                last = {}
                for fn, ws, inc, n in sched.ops[name]:
                    for t, v in ws:
                        if last.get(t.name, -1) < v:
                            engine.wait_ge(t.h, v)
                            last[t.name] = v
                    ins = fn(engine)
                    if inc is not None:
                        ins.then_inc(inc.h, n)
            return run

        block.sync(runner("sync"))
        block.tensor(runner("tensor"))
        block.vector(runner("vector"))
        block.scalar(runner("scalar"))
        block.gpsimd(runner("gpsimd"))

    return nc


def kernel(**inputs):
    x = np.asarray(inputs["L0_attn_x"], dtype=np.float32)
    Wu = np.asarray(inputs["L0_attn_Wu"], dtype=np.float32)
    bu = np.asarray(inputs["L0_attn_bu"], dtype=np.float32)
    Wv = np.asarray(inputs["L0_attn_Wv"], dtype=np.float32)
    bv = np.asarray(inputs["L0_attn_bv"], dtype=np.float32)
    Wg1 = np.asarray(inputs["L0_attn_Wg1"], dtype=np.float32)
    bg1 = np.asarray(inputs["L0_attn_bg1"], dtype=np.float32)
    Wg2 = np.asarray(inputs["L0_attn_Wg2"], dtype=np.float32)
    bg2 = np.asarray(inputs["L0_attn_bg2"], dtype=np.float32)
    cw = np.asarray(inputs["L0_attn_conv_w"], dtype=np.float32)
    cb = np.asarray(inputs["L0_attn_conv_b"], dtype=np.float32)
    up = np.asarray(inputs["L0_ffn_up_kernel"], dtype=np.float32)
    ub = np.asarray(inputs["L0_ffn_up_bias"], dtype=np.float32)
    dn = np.asarray(inputs["L0_ffn_down_kernel"], dtype=np.float32)
    db = np.asarray(inputs["L0_ffn_down_bias"], dtype=np.float32)

    pcol = lambda v: np.ascontiguousarray(v.reshape(-1, 128).T)
    W2u = np.ascontiguousarray(Wu.T.reshape(16, 128, 16, 128).transpose(2, 1, 0, 3))
    W2v = np.ascontiguousarray(Wv.T.reshape(16, 128, 16, 128).transpose(2, 1, 0, 3))
    Wg1T = np.ascontiguousarray(Wg1.T)
    Wg2T = np.ascontiguousarray(Wg2.T)
    upT = np.ascontiguousarray(up.T)
    dnT = np.ascontiguousarray(dn.T)
    cw_p = np.empty((128, 48), dtype=np.float32)
    for k in range(3):
        for dtt in range(16):
            cw_p[:, k * 16 + dtt] = cw[dtt * 128:(dtt + 1) * 128, 0, k]
    ident = np.eye(128, dtype=ml_dtypes.bfloat16)

    shared = {
        "W2u": W2u, "W2v": W2v, "Wg1T": Wg1T, "Wg2T": Wg2T,
        "bu_p": pcol(bu), "bv_p": pcol(bv), "bg1_p": pcol(bg1), "bg2_p": pcol(bg2),
        "cb_p": pcol(cb), "db_p": pcol(db), "ub_p": pcol(ub), "cw_p": cw_p,
        "ident": ident,
    }
    in_maps = []
    for c in range(NCORES):
        m = dict(shared)
        m["x_tok"] = np.ascontiguousarray(x[c])
        m["xT"] = np.ascontiguousarray(x[c].T)
        m["upT_sl"] = np.ascontiguousarray(upT[c * 1024:(c + 1) * 1024, :])
        m["dnT_sl"] = np.ascontiguousarray(dnT[c * 256:(c + 1) * 256, :])
        in_maps.append(m)

    nc = build_nc()
    res = run_bass_kernel_spmd(nc, in_maps, core_ids=list(range(NCORES)))
    out = np.stack([np.ascontiguousarray(res.results[c]["outT"].T) for c in range(NCORES)], axis=0)
    return out


# version 29
# speedup vs baseline: 1.1093x; 1.0094x over previous
import contextlib
import numpy as np
import ml_dtypes
import concourse.bass as bass
import concourse.mybir as mybir
from concourse.bass_utils import run_bass_kernel_spmd

F32 = mybir.dt.float32
BF16 = mybir.dt.bfloat16
I32 = mybir.dt.int32
AF = mybir.ActivationFunctionType
ALU = mybir.AluOpType
AX = mybir.AxisListType

L, D, R, DH, FF = 4096, 2048, 2048, 512, 8192
LS = 512
NLS = L // LS
DT, RT, HTN, FT = D // 128, R // 128, DH // 128, FF // 128
NCORES = 8
EPS = 1e-5
GELU_C = 0.044715
GELU_S = 2.0 * 0.7978845608028654
SGN_MASK = -0x80000000
EXP_MASK = 0x7F800000
INV_BASE = 0x7F000000
MIDS = [0.25, 0.75, 1.25, 1.75, 2.5, 3.5, 5.0]

ENGS = ("sync", "tensor", "vector", "scalar", "gpsimd")


class Trk:
    __slots__ = ("h", "c", "name")

    def __init__(self, h, name):
        self.h = h
        self.c = 0
        self.name = name


class Sched:

    def __init__(self, vtrk=None):
        self.ops = {k: [] for k in ENGS}
        self.vtrk = vtrk

    def add(self, eng, fn, waits=(), inc=None, n=1):
        ws = [(t, v) for (t, v) in waits if v > 0]
        if eng == "vector" and self.vtrk is not None:
            if self.vtrk.c > 0:
                ws.append((self.vtrk, self.vtrk.c))
            if inc is None:
                inc, n = self.vtrk, 1
            elif inc is not self.vtrk:
                raise ValueError("vector ops must inc the vector track")
            else:
                n = 1
        if inc is not None:
            inc.c += n
        self.ops[eng].append((fn, ws, inc, n))


def aft(trk):
    return (trk, trk.c)


def build_nc():
    nc = bass.Bass("TRN2", target_bir_lowering=False)
    ap = lambda t: t.ap()

    d_in = {}
    for name, shape, dt in [
        ("x_tok", [L, D], F32),
        ("xT", [D, L], F32),
        ("W2u", [16, 128, 16, 128], F32),
        ("W2v", [16, 128, 16, 128], F32),
        ("Wg1T", [D, DH], F32),
        ("Wg2T", [DH, R], F32),
        ("upT_sl", [FF // 8, D], F32),
        ("dnT_sl", [D // 8, FF], F32),
        ("bu_p", [128, 16], F32),
        ("bv_p", [128, 16], F32),
        ("bg1_p", [128, 4], F32),
        ("bg2_p", [128, 16], F32),
        ("cb_p", [128, 16], F32),
        ("db_p", [128, 16], F32),
        ("ub_p", [128, 64], F32),
        ("cw_p", [128, 48], F32),
        ("ident", [128, 128], BF16),
    ]:
        d_in[name] = ap(nc.dram_tensor(name, shape, dt, kind="ExternalInput"))
    d_out = ap(nc.dram_tensor("outT", [D, L], F32, kind="ExternalOutput"))

    xnT_scr = ap(nc.dram_tensor("xnT_scr", [32, 16, 128, 128], BF16))
    s_scr = ap(nc.dram_tensor("s_scr", [128, 16, L], F32))
    w2u_scr = ap(nc.dram_tensor("w2u_scr", [16, 128, 2048], BF16))
    w2v_scr = ap(nc.dram_tensor("w2v_scr", [16, 128, 2048], BF16))
    q_up_in = ap(nc.dram_tensor("q_up_in", [1024, 2048], BF16))
    q_dn_in = ap(nc.dram_tensor("q_dn_in", [256, 8192], BF16))
    q_up = ap(nc.dram_tensor("q_up", [8192, 2048], BF16))
    q_dn = ap(nc.dram_tensor("q_dn", [2048, 8192], BF16))

    with contextlib.ExitStack() as stack:
        e = stack.enter_context

        tS = Trk(e(nc.semaphore("tS")), "tS")
        tWU = [Trk(e(nc.semaphore(f"tWU{i}")), f"tWU{i}") for i in range(3)]
        tWV = [Trk(e(nc.semaphore(f"tWV{i}")), f"tWV{i}") for i in range(3)]
        tWZ = [Trk(e(nc.semaphore(f"tWZ{i}")), f"tWZ{i}") for i in range(3)]
        tUPs = [Trk(e(nc.semaphore(f"tUPs{i}")), f"tUPs{i}") for i in range(3)]
        tDNs = [Trk(e(nc.semaphore(f"tDNs{i}")), f"tDNs{i}") for i in range(2)]
        tXN = Trk(e(nc.semaphore("tXN")), "tXN")
        tXNh = Trk(e(nc.semaphore("tXNh")), "tXNh")
        tXF = Trk(e(nc.semaphore("tXF")), "tXF")
        tSFs = [Trk(e(nc.semaphore(f"tSF{i}")), f"tSF{i}") for i in range(2)]
        tQS = Trk(e(nc.semaphore("tQS")), "tQS")
        tACC = Trk(e(nc.semaphore("tACC")), "tACC")
        tCV = [Trk(e(nc.semaphore(f"tCV{i}")), f"tCV{i}") for i in range(2)]
        tCS = [Trk(e(nc.semaphore(f"tCS{i}")), f"tCS{i}") for i in range(2)]
        tG = Trk(e(nc.semaphore("tG")), "tG")
        tC = Trk(e(nc.semaphore("tC")), "tC")
        tT = Trk(e(nc.semaphore("tT")), "tT")
        tV = Trk(e(nc.semaphore("tV")), "tV")
        tSc = Trk(e(nc.semaphore("tSc")), "tSc")
        tG2 = Trk(e(nc.semaphore("tG2")), "tG2")

        sched = Sched(vtrk=tV)
        S = sched.add

        pH = e(nc.psum_tensor("pH", [128, 1024], F32))
        pU = [e(nc.psum_tensor(f"pU{i}", [128, 512], F32)) for i in range(2)]
        pV = [e(nc.psum_tensor(f"pV{i}", [128, 512], F32)) for i in range(2)]
        pG = [e(nc.psum_tensor(f"pG{i}", [128, 512], F32)) for i in range(2)]

        ident = e(nc.sbuf_tensor("identb", [128, 128], BF16))
        bu_p = e(nc.sbuf_tensor("bu_pb", [128, 16], F32))
        bv_p = e(nc.sbuf_tensor("bv_pb", [128, 16], F32))
        bg1_p = e(nc.sbuf_tensor("bg1_pb", [128, 4], F32))
        bg2_p = e(nc.sbuf_tensor("bg2_pb", [128, 16], F32))
        cb_p = e(nc.sbuf_tensor("cb_pb", [128, 16], F32))
        db_p = e(nc.sbuf_tensor("db_pb", [128, 16], F32))
        ub_p = e(nc.sbuf_tensor("ub_pb", [128, 64], F32))
        cw_p = e(nc.sbuf_tensor("cw_pb", [128, 48], F32))

        for name, buf in [("ident", ident), ("bu_p", bu_p), ("bv_p", bv_p),
                          ("bg1_p", bg1_p), ("bg2_p", bg2_p), ("cb_p", cb_p),
                          ("db_p", db_p), ("ub_p", ub_p), ("cw_p", cw_p)]:
            S("sync", lambda e_, a=d_in[name], b=buf: e_.dma_start(out=b[:], in_=a),
              inc=tS, n=16)
        small_ld = aft(tS)

        S("gpsimd", lambda e_: e_.dma_start(out=d_out, in_=d_in["xT"]), inc=tG, n=16)
        prefill_done = aft(tG)

        with contextlib.ExitStack() as p0:
            e0 = p0.enter_context
            cvt = [e0(nc.sbuf_tensor(f"cvt{i}", [128, 2048], BF16)) for i in range(2)]
            xln = [e0(nc.sbuf_tensor(f"xln{i}", [128, 2048], F32)) for i in range(2)]
            xsq = [e0(nc.sbuf_tensor(f"xsq{i}", [128, 2048], F32)) for i in range(2)]
            xnbf = [e0(nc.sbuf_tensor(f"xnbf{i}", [128, 2048], BF16)) for i in range(2)]
            stg = [e0(nc.sbuf_tensor(f"stg{i}", [128, 2048], BF16)) for i in range(2)]
            ls1 = e0(nc.sbuf_tensor("ls1", [128, 1], F32))
            ls2 = e0(nc.sbuf_tensor("ls2", [128, 1], F32))
            lm = e0(nc.sbuf_tensor("lm", [128, 1], F32))
            lv = e0(nc.sbuf_tensor("lv", [128, 1], F32))
            lsd = e0(nc.sbuf_tensor("lsd", [128, 1], F32))
            lrs = e0(nc.sbuf_tensor("lrs", [128, 1], F32))

            cvt_st = {}
            blocks = [(d_in["W2u"], w2u_scr, bt) for bt in range(16)] + \
                     [(d_in["W2v"], w2v_scr, bt) for bt in range(16)]
            for i, (srct, dst, bt) in enumerate(blocks):
                sl = i % 2
                w = [cvt_st[sl]] if sl in cvt_st else []
                S("gpsimd", lambda e_, a=srct, b=cvt[sl], t=bt: e_.dma_start(
                    out=b[:], in_=a[t].rearrange("p a b -> p (a b)")), waits=w, inc=tCV[sl], n=16)
                S("sync", lambda e_, a=cvt[sl], b=dst, t=bt: e_.dma_start(
                    out=b[t], in_=a[:]), waits=[aft(tCV[sl])],
                  inc=tCS[sl], n=16)
                cvt_st[sl] = aft(tCS[sl])
            wcvt_done_S = aft(tCS[0])
            wcvt_done_S2 = aft(tCS[1])

            xld = {}
            xn_done = {}
            tr_done = {}
            ev_done = {}
            st_done = {}
            sqd = {}
            sqrt_done = {}
            for lt in range(32):
                sl = lt % 2
                w = [st_done[lt - 2]] if lt - 2 in st_done else []
                if lt - 2 in xn_done:
                    w.append(xn_done[lt - 2])
                if lt - 2 in sqd:
                    w.append(sqd[lt - 2])
                S("sync", lambda e_, b=xln[sl], t=lt: e_.dma_start(
                    out=b[:], in_=d_in["x_tok"][t * 128:(t + 1) * 128, :]),
                  waits=w, inc=tS, n=16)
                xld[lt] = aft(tS)
                S("vector", lambda e_, b=xln[sl]: e_.reduce_sum(ls1[:], b[:], AX.X),
                  waits=[xld[lt]])
                S("scalar", lambda e_, b=xln[sl], o=xsq[sl]: e_.activation(
                    o[:], b[:], AF.Square),
                  waits=[xld[lt]] + ([xn_done[lt - 2]] if lt - 2 in xn_done else []),
                  inc=tSc)
                sqd[lt] = aft(tSc)
                S("vector", lambda e_, b=xsq[sl]: e_.reduce_sum(ls2[:], b[:], AX.X),
                  waits=[sqd[lt]])
                S("vector", lambda e_: e_.tensor_single_scalar(lm[:], ls1[:], 1.0 / D, ALU.mult))
                S("vector", lambda e_: e_.tensor_scalar(lv[:], ls2[:], 1.0 / D, EPS, ALU.mult, ALU.add),
                  waits=([sqrt_done[lt - 1]] if lt - 1 in sqrt_done else []))
                S("vector", lambda e_: e_.tensor_tensor(ls1[:], lm[:], lm[:], ALU.mult))
                S("vector", lambda e_: e_.tensor_tensor(lv[:], lv[:], ls1[:], ALU.subtract),
                  inc=tV)
                var_done = aft(tV)
                S("scalar", lambda e_: e_.activation(lsd[:], lv[:], AF.Sqrt),
                  waits=[var_done], inc=tSc)
                sqrt_done[lt] = aft(tSc)
                S("vector", lambda e_: e_.reciprocal(lrs[:], lsd[:]), waits=[sqrt_done[lt]])
                S("vector", lambda e_, b=xln[sl], o=xnbf[sl]: e_.tensor_scalar(
                    o[:], b[:], lm[:, 0:1], lrs[:, 0:1], ALU.subtract, ALU.mult),
                  waits=([tr_done[(lt - 2, 3)]] if (lt - 2, 3) in tr_done else []), inc=tV)
                xn_done[lt] = aft(tV)
                pHb = pH[:].bitcast(BF16)
                for gq in range(4):
                    wq = [xn_done[lt]]
                    key = (lt, gq)
                    pkey = (lt, gq - 2) if gq >= 2 else (lt - 1, gq + 2)
                    if pkey in ev_done:
                        wq.append(ev_done[pkey])
                    for j in range(4):
                        dtt = gq * 4 + j
                        S("tensor", lambda e_, src=xnbf[sl], o=pHb, g=gq, jj=j, d2=dtt: e_.transpose(
                            o[:, (g % 2) * 1024 + jj * 128:(g % 2) * 1024 + jj * 128 + 128],
                            src[:, d2 * 128:(d2 + 1) * 128], ident[:]),
                          waits=(wq if j == 0 else []), inc=(tT if j == 3 else None))
                    tr_done[key] = aft(tT)
                    S("scalar", lambda e_, o=stg[sl], g=gq, ph=pHb: e_.copy(
                        o[:, g * 512:(g + 1) * 512], ph[:, (g % 2) * 1024:(g % 2) * 1024 + 512]),
                      waits=[tr_done[key]], inc=tSc)
                    ev_done[key] = aft(tSc)
                S("sync", lambda e_, b=stg[sl], t=lt: e_.dma_start(
                    out=xnT_scr[t].rearrange("a p c -> p a c"),
                    in_=b[:].rearrange("p (a c) -> p a c", c=128)),
                  waits=[ev_done[(lt, 3)]], inc=tS, n=16)
                st_done[lt] = aft(tS)
            xnT_all_stored = aft(tS)

            with contextlib.ExitStack() as q0:
                eq = q0.enter_context
                qbuf = {}
                for en in ("vector", "gpsimd"):
                    qbuf[en] = dict(
                        qin=[eq(nc.sbuf_tensor(f"qin_{en}{i}", [128, 512], F32)) for i in range(2)],
                        z=eq(nc.sbuf_tensor(f"z_{en}", [128, 512], F32)),
                        za=eq(nc.sbuf_tensor(f"za_{en}", [128, 512], F32)),
                        cnt=[eq(nc.sbuf_tensor(f"cnt_{en}{i}", [128, 512], F32)) for i in range(2)],
                        b1=eq(nc.sbuf_tensor(f"b1_{en}", [128, 512], F32)),
                        b2=eq(nc.sbuf_tensor(f"b2_{en}", [128, 512], F32)),
                        b3=eq(nc.sbuf_tensor(f"b3_{en}", [128, 512], F32)),
                        sg=eq(nc.sbuf_tensor(f"sg_{en}", [128, 512], I32)),
                        qo=[eq(nc.sbuf_tensor(f"qo_{en}{i}", [128, 512], BF16)) for i in range(2)],
                        am=eq(nc.sbuf_tensor(f"am_{en}", [128, 16], F32)),
                        eb=eq(nc.sbuf_tensor(f"eb_{en}", [128, 16], I32)),
                        isc=eq(nc.sbuf_tensor(f"isc_{en}", [128, 16], F32)),
                        scl=eq(nc.sbuf_tensor(f"scl_{en}", [128, 16], F32)),
                    )
                qstg = [eq(nc.sbuf_tensor(f"qstg{i}", [128, 512], BF16)) for i in range(2)]

                tiles = []
                for pt in range(8):
                    for dc in range(4):
                        tiles.append(("up", pt, dc))
                for pt in range(2):
                    for fc in range(16):
                        tiles.append(("dn", pt, fc))

                qld = {}
                qdone = {}
                trq_done = {}
                evq_done = {}
                stq_done = {}
                eng_slot = {"vector": 0, "gpsimd": 0}
                for ti, (kind, pt, cc) in enumerate(tiles):
                    en = "vector"
                    tE = tV if en == "vector" else tG2
                    bb = qbuf[en]
                    sl = eng_slot[en] % 2
                    eng_slot[en] += 1
                    if kind == "up":
                        src = d_in["upT_sl"][pt * 128:(pt + 1) * 128, cc * 512:(cc + 1) * 512]
                    else:
                        src = d_in["dnT_sl"][pt * 128:(pt + 1) * 128, cc * 512:(cc + 1) * 512]
                    w = []
                    prev_same = [k for k in range(ti) if (("vector" if k % 3 != 2 else "gpsimd") == en)]
                    if len(prev_same) >= 2:
                        w = [qdone[prev_same[-2]]]
                    S("sync", lambda e_, a=src, b=bb["qin"][sl]: e_.dma_start(out=b[:], in_=a),
                      waits=w, inc=tS, n=16)
                    qld[ti] = aft(tS)
                    qin, z, za, cnt = bb["qin"][sl], bb["z"], bb["za"], bb["cnt"]
                    b1, b2, b3, sg, qo = bb["b1"], bb["b2"], bb["b3"], bb["sg"], bb["qo"][sl]
                    am, eb, isc, scl = bb["am"], bb["eb"], bb["isc"], bb["scl"]
                    q3 = lambda t: t[:].rearrange("p (b k) -> p b k", k=32)
                    wr = [qld[ti]]
                    if en != "vector" and prev_same:
                        wr.append(qdone[prev_same[-1]])
                    S("vector", lambda e_, q=qin, o=am: e_.tensor_reduce(
                        o[:], q3(q), AX.X, ALU.max, apply_absolute_value=True),
                      waits=wr)
                    S("vector", lambda e_, o=am: e_.tensor_single_scalar(o[:], o[:], 1e-30, ALU.max))
                    S("vector", lambda e_, a=am, o=eb: e_.tensor_single_scalar(
                        o[:], a[:].bitcast(I32), EXP_MASK, ALU.bitwise_and))
                    S("vector", lambda e_, a=eb, o=scl: e_.tensor_single_scalar(
                        o[:], a[:].bitcast(F32), 0.25, ALU.mult))
                    S("vector", lambda e_, a=scl, o=isc: e_.reciprocal(o[:], a[:]))
                    S("vector", lambda e_, q=qin, o=sg: e_.tensor_single_scalar(
                        o[:], q[:].bitcast(I32), SGN_MASK, ALU.bitwise_and), inc=tV)
                    scales_done = aft(tV)
                    S(en, lambda e_, q=qin, a=isc, o=z: e_.tensor_tensor(
                        q3(o), q3(q), a[:].broadcast_to([128, 16, 32]), ALU.mult),
                      waits=([scales_done] if en != "vector" else []), inc=tE)
                    z_done = aft(tE)
                    S("scalar", lambda e_, a=z, o=za: e_.activation(o[:], a[:], AF.Abs),
                      waits=[z_done], inc=tSc)
                    za_done = aft(tSc)
                    S(en, lambda e_, a=za, o=cnt[0]: e_.tensor_single_scalar(
                        o[:], a[:], MIDS[0], ALU.is_ge), waits=[za_done])
                    for mi, mid in enumerate(MIDS[1:]):
                        src_c, dst_c = cnt[mi % 2], cnt[(mi + 1) % 2]
                        S(en, lambda e_, a=src_c, b=dst_c, m=mid, x=za: e_.scalar_tensor_tensor(
                            b[:], x[:], m, a[:], ALU.is_ge, ALU.add))
                    cfin = cnt[(len(MIDS) - 1) % 2]
                    S(en, lambda e_, c=cfin, o=b1: e_.tensor_single_scalar(o[:], c[:], 0.5, ALU.mult))
                    S(en, lambda e_, c=cfin, o=b2: e_.tensor_single_scalar(o[:], c[:], -2.0, ALU.add))
                    S(en, lambda e_, a=b1, b=b2, o=b3: e_.tensor_tensor(o[:], a[:], b[:], ALU.max))
                    S(en, lambda e_, c=cfin, o=b1: e_.tensor_scalar(
                        o[:], c[:], 2.0, -8.0, ALU.mult, ALU.add))
                    S(en, lambda e_, a=b3, b=b1, o=b2: e_.tensor_tensor(o[:], a[:], b[:], ALU.max))
                    S(en, lambda e_, a=b2, o=b1: e_.tensor_single_scalar(o[:], a[:], 6.0, ALU.min))
                    S(en, lambda e_, a=b1, s2=scl, o=b2: e_.tensor_tensor(
                        o[:].rearrange("p (b k) -> p b k", k=32),
                        a[:].rearrange("p (b k) -> p b k", k=32),
                        s2[:].broadcast_to([128, 16, 32]), ALU.mult), inc=tE)
                    deq_done = aft(tE)
                    S("vector", lambda e_, a=b2, b=sg, o=b3: e_.tensor_tensor(
                        o[:].bitcast(I32), a[:].bitcast(I32), b[:], ALU.bitwise_or),
                      waits=([deq_done] if en != "vector" else []))
                    S("vector", lambda e_, a=b3, o=qo: e_.tensor_copy(o[:], a[:]), inc=tV)
                    qdone[ti] = aft(tV)
                    pq = pU[ti % 2][:].bitcast(BF16)
                    wq = [qdone[ti]]
                    if ti - 2 in evq_done:
                        wq.append(evq_done[ti - 2])
                    for j in range(4):
                        S("tensor", lambda e_, o=pq, src=qo, jj=j: e_.transpose(
                            o[:, jj * 128:(jj + 1) * 128], src[:, jj * 128:(jj + 1) * 128], ident[:]),
                          waits=(wq if j == 0 else []), inc=(tT if j == 3 else None))
                    trq_done[ti] = aft(tT)
                    wq = [trq_done[ti]]
                    if ti - 2 in stq_done:
                        wq.append(stq_done[ti - 2])
                    S("scalar", lambda e_, o=qstg[ti % 2], p=pq: e_.copy(o[:], p[:, 0:512]),
                      waits=wq, inc=tSc)
                    evq_done[ti] = aft(tSc)
                    if kind == "up":
                        dst = q_up_in[pt * 128:(pt + 1) * 128, cc * 512:(cc + 1) * 512]
                    else:
                        dst = q_dn_in[pt * 128:(pt + 1) * 128, cc * 512:(cc + 1) * 512]
                    S("sync", lambda e_, b=qstg[ti % 2], d2=dst: e_.dma_start(
                        out=d2.rearrange("p (j f) -> p j f", f=128),
                        in_=b[:].rearrange("p (j f) -> p j f", f=128)),
                      waits=[evq_done[ti]], inc=tQS, n=16)
                    stq_done[ti] = aft(tQS)
                q_in_stored = aft(tQS)


        p0S, p0T, p0V, p0Sc, p0G, p0G2 = aft(tS), aft(tT), aft(tV), aft(tSc), aft(tG), aft(tG2)
        fence0 = [p0S, p0T, p0V, p0Sc, p0G, p0G2, aft(tQS), aft(tCS[0]), aft(tCS[1])]

        with contextlib.ExitStack() as pr:
            er = pr.enter_context
            Wg1T_sb = er(nc.sbuf_tensor("Wg1T_sb", [128, 16, 512], BF16))
            Wg2T_sb = er(nc.sbuf_tensor("Wg2T_sb", [128, 4, 2048], BF16))
            diag_sb = er(nc.sbuf_tensor("diag_sb", [128, 48, 128], BF16))
            xnT_sb = er(nc.sbuf_tensor("xnT_sb", [128, 16, 768], BF16))
            hT_sb = er(nc.sbuf_tensor("hT_sb", [128, 4, 512], BF16))
            sigT = [er(nc.sbuf_tensor(f"sigT{i}", [128, 512], F32)) for i in range(2)]
            ugT = [er(nc.sbuf_tensor(f"ugT{i}", [128, 512], F32)) for i in range(2)]
            kcT = [er(nc.sbuf_tensor(f"kcT{i}", [128, 512], F32)) for i in range(2)]
            vgT = [er(nc.sbuf_tensor(f"vgT{i}", [128, 512], BF16)) for i in range(2)]
            gvT = er(nc.sbuf_tensor("gvT", [128, 16, 512], BF16))
            s_stage = er(nc.sbuf_tensor("s_stage", [128, 16, 512], F32))
            wu_ring = er(nc.sbuf_tensor("wu_ring", [128, 3, 16, 128], BF16))
            wv_ring = er(nc.sbuf_tensor("wv_ring", [128, 3, 16, 128], BF16))
            wz_ring = er(nc.sbuf_tensor("wz_ring", [128, 3, 16, 128], BF16))
            carry = er(nc.sbuf_tensor("carry", [128, 16], F32))

            S("gpsimd", lambda e_: e_.dma_start(
                out=Wg1T_sb[:], in_=d_in["Wg1T"].rearrange("(b p) h -> p b h", p=128)),
              waits=fence0, inc=tG, n=16)
            S("gpsimd", lambda e_: e_.dma_start(
                out=Wg2T_sb[:], in_=d_in["Wg2T"].rearrange("(b p) r -> p b r", p=128)),
              inc=tG, n=16)
            wg_ld = aft(tG)
            S("gpsimd", lambda e_: e_.collective_compute(
                "AllGather", ALU.bypass, replica_groups=[list(range(NCORES))],
                ins=[q_up_in.opt()], outs=[q_up.opt()]),
              waits=[q_in_stored], inc=tC)
            S("gpsimd", lambda e_: e_.collective_compute(
                "AllGather", ALU.bypass, replica_groups=[list(range(NCORES))],
                ins=[q_dn_in.opt()], outs=[q_dn.opt()]),
              inc=tC)
            cc_done = aft(tC)
            S("vector", lambda e_: e_.memset(carry[:], 0.0), waits=fence0, inc=tV)
            for k in range(3):
                for dtt in range(16):
                    S("vector", lambda e_, kk=k, d2=dtt: e_.tensor_scalar(
                        diag_sb[:, kk * 16 + d2, :], ident[:], cw_p[:, kk * 16 + d2:kk * 16 + d2 + 1],
                        None, ALU.mult), waits=[small_ld], inc=(tV if (k, dtt) == (2, 15) else None))
            diag_done = aft(tV)

            scr_r = xnT_scr.rearrange("a b p c -> p b a c")

            wu_ld, wv_ld, wz_ld = {}, {}, {}
            u_done, v_done, g_done = {}, {}, {}
            sig_done, ug_done, vg_done, gv_done = {}, {}, {}, {}
            uv_done, scp_done = {}, {}
            relu_done = {}
            s_store = {}
            xnT_release = (tT, 0)
            rt_end = {}
            uv_end_prev = (tT, 0)
            g_mms_prev = (tT, 0)
            memset_done = None

            def wu_dma(ls, rt, extra=()):
                slot = rt % 3
                S("sync", lambda e_, r2=rt, s2=slot: e_.dma_start(
                    out=wu_ring[:, s2, :, :].rearrange("p a b -> p (a b)"), in_=w2u_scr[r2]),
                  waits=list(extra), inc=tWU[slot], n=16)
                wu_ld[(ls, rt)] = aft(tWU[slot])
                S("sync", lambda e_, r2=rt, s2=slot: e_.dma_start(
                    out=wv_ring[:, s2, :, :].rearrange("p a b -> p (a b)"), in_=w2v_scr[r2]),
                  inc=tWV[slot], n=16)
                wv_ld[(ls, rt)] = aft(tWV[slot])

            def wz_dma(ls, dtt, extra=()):
                slot = dtt % 3
                S("sync", lambda e_, d2=dtt, s2=slot: e_.dma_start(
                    out=wz_ring[:, s2, :, :].rearrange("p a b -> p (a b)"), in_=w2u_scr[d2]),
                  waits=list(extra), inc=tWZ[slot], n=16)
                wz_ld[(ls, dtt)] = aft(tWZ[slot])

            for ls in range(NLS):
                lt0 = ls * 4
                w = [xnT_release]
                if ls == 0:
                    w.extend(fence0)
                for i in range(4):
                    S("sync", lambda e_, t=lt0 + i, o=128 + i * 128: e_.dma_start(
                        out=xnT_sb[:, :, o:o + 128], in_=scr_r[:, :, t, :]),
                      waits=(w if i == 0 else []), inc=tXN, n=16)
                if ls > 0:
                    S("sync", lambda e_, t=lt0 - 1: e_.dma_start(
                        out=xnT_sb[:, :, 0:128], in_=scr_r[:, :, t, :]), inc=tXNh, n=16)
                else:
                    S("vector", lambda e_: e_.memset(xnT_sb[:, :, 127:128], 0.0),
                      waits=[xnT_release], inc=tV)
                    memset_done = aft(tV)
                if ls < NLS - 1:
                    S("sync", lambda e_, t=lt0 + 4: e_.dma_start(
                        out=xnT_sb[:, :, 640:768], in_=scr_r[:, :, t, :]), inc=tXNh, n=16)
                else:
                    S("vector", lambda e_: e_.memset(xnT_sb[:, :, 640:641], 0.0),
                      waits=[xnT_release], inc=tV)
                    memset_done = aft(tV)
                xnT_ld = aft(tXN)
                halo_ld = aft(tXNh)

                prev_user = {0: 15, 1: 13, 2: 14}
                if ls == 0:
                    for rt in range(3):
                        wu_dma(ls, rt, extra=[wcvt_done_S, wcvt_done_S2] + fence0)
                    for dtt in range(3):
                        wz_dma(ls, dtt, extra=([wcvt_done_S, wcvt_done_S2] + fence0 if dtt == 0 else []))
                else:
                    for rt in range(3):
                        wu_dma(ls, rt, extra=[rt_end[(ls - 1, prev_user[rt])]])
                    for dtt in range(3):
                        wz_dma(ls, dtt, extra=[uv_done[(ls - 1, prev_user[dtt])]])

                for ht in range(4):
                    half = (ht % 2)
                    w = [xnT_ld]
                    if ls == 0 and ht == 0:
                        w += [wg_ld, p0V, p0Sc, p0G2, diag_done]
                    if (ls, ht - 2) in relu_done:
                        w.append(relu_done[(ls, ht - 2)])
                    if ls > 0 and ht < 2:
                        w.append(relu_done[(ls - 1, ht + 2)])
                    for dtt in range(16):
                        S("tensor", lambda e_, d2=dtt, h2=ht, hf=half: e_.matmul(
                            pH[:, hf * 512:hf * 512 + 512],
                            Wg1T_sb[:, d2, h2 * 128:(h2 + 1) * 128],
                            xnT_sb[:, d2, 128:640],
                            start=(d2 == 0), stop=(d2 == 15)),
                          waits=(w if dtt == 0 else []), inc=(tT if dtt == 15 else None))
                    h_mm = aft(tT)
                    w2 = [h_mm, small_ld]
                    w2.append(g_mms_prev)
                    S("scalar", lambda e_, h2=ht, hf=half: e_.activation(
                        hT_sb[:, h2, :], pH[:, hf * 512:hf * 512 + 512], AF.Relu,
                        bias=bg1_p[:, h2:h2 + 1]), waits=w2, inc=tSc)
                    relu_done[(ls, ht)] = aft(tSc)

                for rt in range(16):
                    par = rt % 2
                    w = [wu_ld[(ls, rt)], xnT_ld]
                    if (ls, rt - 2) in ug_done:
                        w.append(ug_done[(ls, rt - 2)])
                    elif ls > 0 and rt < 2:
                        w.append(ug_done[(ls - 1, rt + 14)])
                    for dtt in range(16):
                        S("tensor", lambda e_, d2=dtt, r2=rt: e_.matmul(
                            pU[r2 % 2][:], wu_ring[:, r2 % 3, d2, :], xnT_sb[:, d2, 128:640],
                            start=(d2 == 0), stop=(d2 == 15)),
                          waits=(w if dtt == 0 else []), inc=(tT if dtt == 15 else None))
                    u_done[(ls, rt)] = aft(tT)
                    w = [wv_ld[(ls, rt)]]
                    if (ls, rt - 2) in vg_done:
                        w.append(vg_done[(ls, rt - 2)])
                    elif ls > 0 and rt < 2:
                        w.append(vg_done[(ls - 1, rt + 14)])
                    for dtt in range(16):
                        S("tensor", lambda e_, d2=dtt, r2=rt: e_.matmul(
                            pV[r2 % 2][:], wv_ring[:, r2 % 3, d2, :], xnT_sb[:, d2, 128:640],
                            start=(d2 == 0), stop=(d2 == 15)),
                          waits=(w if dtt == 0 else []), inc=(tT if dtt == 15 else None))
                    v_done[(ls, rt)] = aft(tT)
                    w = [relu_done[(ls, 3)]]
                    if (ls, rt - 2) in sig_done:
                        w.append(sig_done[(ls, rt - 2)])
                    elif ls > 0 and rt < 2:
                        w.append(sig_done[(ls - 1, rt + 14)])
                    for ht in range(4):
                        S("tensor", lambda e_, h2=ht, r2=rt: e_.matmul(
                            pG[r2 % 2][:], Wg2T_sb[:, h2, r2 * 128:(r2 + 1) * 128], hT_sb[:, h2, :],
                            start=(h2 == 0), stop=(h2 == 3)),
                          waits=(w if ht == 0 else []), inc=(tT if ht == 3 else None))
                    g_done[(ls, rt)] = aft(tT)
                    if rt + 3 < 16:
                        wu_dma(ls, rt + 3, extra=[u_done[(ls, rt)], v_done[(ls, rt)]])
                    w = [g_done[(ls, rt)]]
                    if (ls, rt - 2) in vg_done:
                        w.append(vg_done[(ls, rt - 2)])
                    elif ls > 0 and rt < 2:
                        w.append(vg_done[(ls - 1, rt + 14)])
                    S("scalar", lambda e_, r2=rt: e_.activation(
                        sigT[r2 % 2][:], pG[r2 % 2][:], AF.Sigmoid, bias=bg2_p[:, r2:r2 + 1]),
                      waits=w, inc=tSc)
                    sig_done[(ls, rt)] = aft(tSc)
                    S("vector", lambda e_, r2=rt: e_.scalar_tensor_tensor(
                        ugT[r2 % 2][:], pU[r2 % 2][:], bu_p[:, r2:r2 + 1], sigT[r2 % 2][:],
                        ALU.add, ALU.mult),
                      waits=[u_done[(ls, rt)], sig_done[(ls, rt)]], inc=tV)
                    ug_done[(ls, rt)] = aft(tV)
                    for rep in range(2):
                        S("vector", lambda e_, r2=rt: e_.tensor_tensor_scan(
                            kcT[r2 % 2][:], ugT[r2 % 2][:], ugT[r2 % 2][:], carry[:, r2:r2 + 1],
                            ALU.add, ALU.bypass))
                    S("vector", lambda e_, r2=rt: e_.tensor_copy(
                        carry[:, r2:r2 + 1], kcT[r2 % 2][:, 511:512]))
                    S("vector", lambda e_, r2=rt: e_.scalar_tensor_tensor(
                        vgT[r2 % 2][:], pV[r2 % 2][:], bv_p[:, r2:r2 + 1], sigT[r2 % 2][:],
                        ALU.add, ALU.mult),
                      waits=[v_done[(ls, rt)]], inc=tV)
                    vg_done[(ls, rt)] = aft(tV)
                    wgv = []
                    if rt == 0:
                        wgv.append(uv_end_prev)
                    S("vector", lambda e_, r2=rt: e_.tensor_tensor(
                        gvT[:, r2, :], kcT[r2 % 2][:], vgT[r2 % 2][:], ALU.mult),
                      waits=wgv, inc=tV)
                    gv_done[(ls, rt)] = aft(tV)
                    rt_end[(ls, rt)] = aft(tT)
                g_mms_prev = aft(tT)

                for dtt in range(16):
                    par = dtt % 2
                    w0 = [wz_ld[(ls, dtt)], vg_done[(ls, 14 + par)]]
                    if dtt == 0:
                        w0.append(halo_ld)
                    if (ls, dtt - 2) in scp_done:
                        w0.append(scp_done[(ls, dtt - 2)])
                    if memset_done is not None:
                        w0.append(memset_done)
                    for rt in range(16):
                        w = list(w0) if rt == 0 else [gv_done[(ls, rt)]]
                        if rt == 0:
                            w.append(gv_done[(ls, 0)])
                        S("tensor", lambda e_, d2=dtt, r2=rt: e_.matmul(
                            pV[d2 % 2][:], wz_ring[:, d2 % 3, r2, :], gvT[:, r2, :],
                            start=(r2 == 0), stop=False),
                          waits=w, inc=None)
                    for k in range(3):
                        off = 127 + k
                        S("tensor", lambda e_, d2=dtt, kk=k, o=off: e_.matmul(
                            pV[d2 % 2][:], diag_sb[:, kk * 16 + d2, :], xnT_sb[:, d2, o:o + 512],
                            start=False, stop=(kk == 2)),
                          inc=(tT if k == 2 else None))
                    uv_done[(ls, dtt)] = aft(tT)
                    if dtt + 3 < 16:
                        wz_dma(ls, dtt + 3, extra=[uv_done[(ls, dtt)]])
                    w = [uv_done[(ls, dtt)]]
                    if ls - 1 in s_store:
                        w.append(s_store[ls - 1])
                    S("scalar", lambda e_, d2=dtt: e_.activation(
                        s_stage[:, d2, :], pV[d2 % 2][:], AF.Identity, bias=cb_p[:, d2:d2 + 1]),
                      waits=w, inc=tSc)
                    scp_done[(ls, dtt)] = aft(tSc)
                uv_end_prev = aft(tT)
                xnT_release = aft(tT)

                S("sync", lambda e_, l2=ls: e_.dma_start(
                    out=s_scr[:, :, l2 * 512:(l2 + 1) * 512], in_=s_stage[:]),
                  waits=[scp_done[(ls, 15)]], inc=tS, n=16)
                s_store[ls] = aft(tS)

        prS, prT, prV, prSc, prG = aft(tS), aft(tT), aft(tV), aft(tSc), aft(tG)
        fenceR = [prS, prT, prV, prSc, prG, aft(tXN), aft(tXNh)] + \
            [aft(t) for t in tWU + tWV + tWZ]

        with contextlib.ExitStack() as pf:
            ef = pf.enter_context
            xnF = ef(nc.sbuf_tensor("xnF", [128, 16, 512], BF16))
            sF = [ef(nc.sbuf_tensor(f"sF{i}", [128, 16, 512], F32)) for i in range(2)]
            aT = ef(nc.sbuf_tensor("aT", [128, 64, 512], BF16))
            up_ring = ef(nc.sbuf_tensor("up_ring", [128, 3, 16, 128], BF16))
            dn_ring = ef(nc.sbuf_tensor("dn_ring", [128, 2, 64, 128], BF16))
            xgf = [ef(nc.sbuf_tensor(f"xgf{i}", [128, 512], F32)) for i in range(2)]
            gA = [ef(nc.sbuf_tensor(f"gA{i}", [128, 512], F32)) for i in range(2)]
            gB = [ef(nc.sbuf_tensor(f"gB{i}", [128, 512], F32)) for i in range(2)]
            gt = [ef(nc.sbuf_tensor(f"gt{i}", [128, 512], F32)) for i in range(2)]
            gt2 = ef(nc.sbuf_tensor("gt2", [128, 512], F32))

            up_ld, dn_ld = {}, {}
            upm_done, a_done = {}, {}
            dnm_done, silu_done, sg_done = {}, {}, {}
            acc_done = {}
            xn_ld, s_ld = {}, {}
            up_phase_end = (tT, 0)
            dn_phase_end = (tT, 0)

            def up_dma(ls, ft, extra=()):
                S("sync", lambda e_, f2=ft: e_.dma_start(
                    out=up_ring[:, f2 % 3, :, :].rearrange("p a b -> p (a b)"),
                    in_=q_up[f2 * 128:(f2 + 1) * 128, :]),
                  waits=list(extra), inc=tUPs[ft % 3], n=16)
                up_ld[(ls, ft)] = aft(tUPs[ft % 3])

            def dn_dma(ls, dtt, extra=()):
                S("sync", lambda e_, d2=dtt: e_.dma_start(
                    out=dn_ring[:, d2 % 2, :, :].rearrange("p a b -> p (a b)"),
                    in_=q_dn[d2 * 128:(d2 + 1) * 128, :]),
                  waits=list(extra), inc=tDNs[dtt % 2], n=16)
                dn_ld[(ls, dtt)] = aft(tDNs[dtt % 2])

            for ls in range(NLS):
                lt0 = ls * 4
                w = [up_phase_end]
                if ls == 0:
                    w += fenceR + [cc_done]
                for i in range(4):
                    S("sync", lambda e_, t=lt0 + i, o=i * 128: e_.dma_start(
                        out=xnF[:, :, o:o + 128], in_=scr_r[:, :, t, :]),
                      waits=(w if i == 0 else []), inc=tXF, n=16)
                xn_ld[ls] = aft(tXF)
                w = []
                if ls - 2 in acc_done:
                    w.append(acc_done[ls - 2])
                if ls == 0:
                    w += fenceR
                S("sync", lambda e_, l2=ls, b=sF[ls % 2]: e_.dma_start(
                    out=b[:], in_=s_scr[:, :, l2 * 512:(l2 + 1) * 512]),
                  waits=w, inc=tSFs[ls % 2], n=16)
                s_ld[ls] = aft(tSFs[ls % 2])

                prev_user_up = {0: 63, 1: 61, 2: 62}
                if ls == 0:
                    for ft in range(3):
                        up_dma(ls, ft, extra=([cc_done] + fenceR if ft == 0 else []))
                else:
                    for ft in range(3):
                        up_dma(ls, ft, extra=[upm_done[(ls - 1, prev_user_up[ft])]])

                for ft in range(64):
                    par = ft % 2
                    w = [up_ld[(ls, ft)], xn_ld[ls]]
                    if (ls, ft - 2) in a_done:
                        w.append(a_done[(ls, ft - 2)])
                    elif ls > 0 and ft < 2:
                        w.append(a_done[(ls - 1, ft + 62)])
                    for dtt in range(16):
                        S("tensor", lambda e_, d2=dtt, f2=ft: e_.matmul(
                            pU[f2 % 2][:], up_ring[:, f2 % 3, d2, :], xnF[:, d2, :],
                            start=(d2 == 0), stop=(d2 == 15)),
                          waits=(w if dtt == 0 else []), inc=(tT if dtt == 15 else None))
                    upm_done[(ls, ft)] = aft(tT)
                    if ft + 3 < 64:
                        up_dma(ls, ft + 3, extra=[upm_done[(ls, ft)]])
                    w = [upm_done[(ls, ft)]]
                    if (ls, ft - 2) in a_done:
                        w.append(a_done[(ls, ft - 2)])
                    elif ls > 0 and ft < 2:
                        w.append(a_done[(ls - 1, ft + 62)])
                    S("scalar", lambda e_, f2=ft: e_.activation(
                        xgf[f2 % 2][:], pU[f2 % 2][:], AF.Identity, bias=ub_p[:, f2:f2 + 1]),
                      waits=w, inc=tSc)
                    xg_done = aft(tSc)
                    S("vector", lambda e_, f2=ft: e_.tensor_tensor(
                        gA[f2 % 2][:], xgf[f2 % 2][:], xgf[f2 % 2][:], ALU.mult),
                      waits=[xg_done])
                    S("vector", lambda e_, f2=ft: e_.tensor_scalar(
                        gB[f2 % 2][:], gA[f2 % 2][:], GELU_C * GELU_S, GELU_S, ALU.mult, ALU.add))
                    S("vector", lambda e_, f2=ft: e_.tensor_tensor(
                        gA[f2 % 2][:], gB[f2 % 2][:], xgf[f2 % 2][:], ALU.mult), inc=tV)
                    u_in_done = aft(tV)
                    S("scalar", lambda e_, f2=ft: e_.activation(
                        gB[f2 % 2][:], gA[f2 % 2][:], AF.Sigmoid),
                      waits=[u_in_done], inc=tSc)
                    sg_act = aft(tSc)
                    wv2 = [sg_act]
                    if ft == 0:
                        wv2.append(dn_phase_end)
                    S("vector", lambda e_, f2=ft: e_.tensor_tensor(
                        aT[:, f2, :], xgf[f2 % 2][:], gB[f2 % 2][:], ALU.mult),
                      waits=wv2, inc=tV)
                    a_done[(ls, ft)] = aft(tV)
                up_phase_end = aft(tT)

                if ls == 0:
                    dn_dma(ls, 0, extra=[cc_done] + fenceR)
                    dn_dma(ls, 1)
                else:
                    dn_dma(ls, 0, extra=[dnm_done[(ls - 1, 14)]])
                    dn_dma(ls, 1, extra=[dnm_done[(ls - 1, 15)]])
                for dtt in range(16):
                    par = dtt % 2
                    w0 = [dn_ld[(ls, dtt)]]
                    if (ls, dtt - 2) in sg_done:
                        w0.append(sg_done[(ls, dtt - 2)])
                    elif ls > 0 and dtt < 2:
                        w0.append(sg_done[(ls - 1, dtt + 14)])
                    for ft in range(64):
                        w = (w0 + [a_done[(ls, ft)]]) if ft == 0 else [a_done[(ls, ft)]]
                        S("tensor", lambda e_, d2=dtt, f2=ft: e_.matmul(
                            pV[d2 % 2][:], dn_ring[:, d2 % 2, f2, :], aT[:, f2, :],
                            start=(f2 == 0), stop=(f2 == 63)),
                          waits=w, inc=(tT if ft == 63 else None))
                    dnm_done[(ls, dtt)] = aft(tT)
                    if dtt + 2 < 16:
                        dn_dma(ls, dtt + 2, extra=[dnm_done[(ls, dtt)]])
                    w = [dnm_done[(ls, dtt)]]
                    if (ls, dtt - 2) in sg_done:
                        w.append(sg_done[(ls, dtt - 2)])
                    S("scalar", lambda e_, d2=dtt: e_.activation(
                        gt[d2 % 2][:], pV[d2 % 2][:], AF.Sigmoid, bias=db_p[:, d2:d2 + 1]),
                      waits=w, inc=tSc)
                    sig_dn = aft(tSc)
                    S("vector", lambda e_, d2=dtt: e_.scalar_tensor_tensor(
                        gt2[:], pV[d2 % 2][:], db_p[:, d2:d2 + 1], gt[d2 % 2][:],
                        ALU.add, ALU.mult),
                      waits=[sig_dn, s_ld[ls]])
                    S("vector", lambda e_, d2=dtt, b=sF[ls % 2]: e_.tensor_tensor(
                        b[:, d2, :], b[:, d2, :], gt2[:], ALU.mult), inc=tV)
                    sg_done[(ls, dtt)] = aft(tV)
                dn_phase_end = aft(tT)

                w = [sg_done[(ls, 15)], prefill_done]
                S("gpsimd", lambda e_, l2=ls, b=sF[ls % 2]: e_.dma_start(
                    out=d_out.rearrange("(b p) l -> p b l", p=128)[:, :, l2 * 512:(l2 + 1) * 512],
                    in_=b[:], accum_op=ALU.add),
                  waits=w, inc=tACC, n=16)
                acc_done[ls] = aft(tACC)

        S("gpsimd", lambda e_: e_.engine_nop(), waits=[aft(tG), aft(tACC)])
        S("sync", lambda e_: e_.nop(), waits=[aft(tS), aft(tG), aft(tC), aft(tXF), aft(tXNh)] +
          [aft(t) for t in tUPs + tDNs + tCV + tCS + tSFs])

        block = e(nc.Block())

        def runner(name):
            def run(engine):
                last = {}
                for fn, ws, inc, n in sched.ops[name]:
                    for t, v in ws:
                        if last.get(t.name, -1) < v:
                            engine.wait_ge(t.h, v)
                            last[t.name] = v
                    ins = fn(engine)
                    if inc is not None:
                        ins.then_inc(inc.h, n)
            return run

        block.sync(runner("sync"))
        block.tensor(runner("tensor"))
        block.vector(runner("vector"))
        block.scalar(runner("scalar"))
        block.gpsimd(runner("gpsimd"))

    return nc


def kernel(**inputs):
    x = np.asarray(inputs["L0_attn_x"], dtype=np.float32)
    Wu = np.asarray(inputs["L0_attn_Wu"], dtype=np.float32)
    bu = np.asarray(inputs["L0_attn_bu"], dtype=np.float32)
    Wv = np.asarray(inputs["L0_attn_Wv"], dtype=np.float32)
    bv = np.asarray(inputs["L0_attn_bv"], dtype=np.float32)
    Wg1 = np.asarray(inputs["L0_attn_Wg1"], dtype=np.float32)
    bg1 = np.asarray(inputs["L0_attn_bg1"], dtype=np.float32)
    Wg2 = np.asarray(inputs["L0_attn_Wg2"], dtype=np.float32)
    bg2 = np.asarray(inputs["L0_attn_bg2"], dtype=np.float32)
    cw = np.asarray(inputs["L0_attn_conv_w"], dtype=np.float32)
    cb = np.asarray(inputs["L0_attn_conv_b"], dtype=np.float32)
    up = np.asarray(inputs["L0_ffn_up_kernel"], dtype=np.float32)
    ub = np.asarray(inputs["L0_ffn_up_bias"], dtype=np.float32)
    dn = np.asarray(inputs["L0_ffn_down_kernel"], dtype=np.float32)
    db = np.asarray(inputs["L0_ffn_down_bias"], dtype=np.float32)

    pcol = lambda v: np.ascontiguousarray(v.reshape(-1, 128).T)
    W2u = np.ascontiguousarray(Wu.T.reshape(16, 128, 16, 128).transpose(2, 1, 0, 3))
    W2v = np.ascontiguousarray(Wv.T.reshape(16, 128, 16, 128).transpose(2, 1, 0, 3))
    Wg1T = np.ascontiguousarray(Wg1.T)
    Wg2T = np.ascontiguousarray(Wg2.T)
    upT = np.ascontiguousarray(up.T)
    dnT = np.ascontiguousarray(dn.T)
    cw_p = np.empty((128, 48), dtype=np.float32)
    for k in range(3):
        for dtt in range(16):
            cw_p[:, k * 16 + dtt] = cw[dtt * 128:(dtt + 1) * 128, 0, k]
    ident = np.eye(128, dtype=ml_dtypes.bfloat16)

    shared = {
        "W2u": W2u, "W2v": W2v, "Wg1T": Wg1T, "Wg2T": Wg2T,
        "bu_p": pcol(bu), "bv_p": pcol(bv), "bg1_p": pcol(bg1), "bg2_p": pcol(bg2),
        "cb_p": pcol(cb), "db_p": pcol(db), "ub_p": pcol(ub), "cw_p": cw_p,
        "ident": ident,
    }
    in_maps = []
    for c in range(NCORES):
        m = dict(shared)
        m["x_tok"] = np.ascontiguousarray(x[c])
        m["xT"] = np.ascontiguousarray(x[c].T)
        m["upT_sl"] = np.ascontiguousarray(upT[c * 1024:(c + 1) * 1024, :])
        m["dnT_sl"] = np.ascontiguousarray(dnT[c * 256:(c + 1) * 256, :])
        in_maps.append(m)

    nc = build_nc()
    res = run_bass_kernel_spmd(nc, in_maps, core_ids=list(range(NCORES)))
    out = np.stack([np.ascontiguousarray(res.results[c]["outT"].T) for c in range(NCORES)], axis=0)
    return out
